# version 10
# baseline (speedup 1.0000x reference)
import contextlib
import numpy as np
import ml_dtypes
import concourse.bass as bass
import concourse.mybir as mybir
from concourse.bass_utils import run_bass_kernel_spmd

F32 = mybir.dt.float32
BF16 = mybir.dt.bfloat16
I32 = mybir.dt.int32
AF = mybir.ActivationFunctionType
ALU = mybir.AluOpType
AX = mybir.AxisListType

NCORES = 8
S = 8192
D = 4096
NQH = 4
TOK = 1024
NT = 8
NDC = 32
NTC = 16
NKT = 64
VIS = 2457
ISQ = np.float32(1.0 / np.sqrt(128.0))
NEG_B = np.float32(-20.0)


def f32(v):
    return np.float32(v)


class Sem:
    def __init__(self, es, nc, name):
        self.h = es.enter_context(nc.semaphore(name))
        self.n = 0
        self.name = name


class Prog:
    def __init__(self, nc, es):
        self.nc = nc
        self.es = es
        self._waited = {}

    def sem(self, name):
        return Sem(self.es, self.nc, name)

    def wge(self, eng, sem, val):
        if val <= 0:
            return
        key = (id(eng), id(sem))
        if self._waited.get(key, -1) >= val:
            return
        eng.wait_ge(sem.h, val)
        self._waited[key] = val

    @staticmethod
    def inc(sem, instr, amt=1):
        instr.then_inc(sem.h, amt)
        sem.n += amt
        return instr


def build(dbg=()):
    nc = bass.Bass("TRN2", target_bir_lowering=False, debug=False)
    nc.detect_race_conditions = False
    es = contextlib.ExitStack()
    p = Prog(nc, es)
    dt = nc.dram_tensor

    x = dt("x", [TOK, D], F32, kind="ExternalInput")
    wqh = dt("wqh", [128, NDC, NQH, 128], F32, kind="ExternalInput")
    wkh = dt("wkh", [128, NDC, 128], F32, kind="ExternalInput")
    wvh = dt("wvh", [128, NDC, 128], F32, kind="ExternalInput")
    w1s = dt("w1s", [3, 32, 128, 32, 128], F32, kind="ExternalInput")
    w2s = dt("w2s", [3, 32, 128, 32, 128], F32, kind="ExternalInput")
    wos = dt("wos", [32, 128, 32, 128], F32, kind="ExternalInput")
    gwh = dt("gwh", [128, NDC, 3], F32, kind="ExternalInput")
    gwt0 = dt("gwt0", [1, 3 * D], F32, kind="ExternalInput")
    gwlast = dt("gwlast", [1, 3], F32, kind="ExternalInput")
    gbin = dt("gbin", [1, 3], F32, kind="ExternalInput")
    b1h = dt("b1h", [128, 96], F32, kind="ExternalInput")
    b2g = dt("b2g", [1, 3 * D], F32, kind="ExternalInput")
    ctabin = dt("ctabin", [128, S], F32, kind="ExternalInput")
    stabin = dt("stabin", [128, S], F32, kind="ExternalInput")
    maskin = dt("maskin", [128, 4, 512], BF16, kind="ExternalInput")
    vmaskin = dt("vmaskin", [128, NT], F32, kind="ExternalInput")
    identin = dt("identin", [128, 128], BF16, kind="ExternalInput")
    identfin = dt("identfin", [128, 128], F32, kind="ExternalInput")
    is0in = dt("is0in", [1, 1], F32, kind="ExternalInput")
    agoffin = dt("agoffin", [1, 1], I32, kind="ExternalInput")

    ag1_in = dt("ag1_in", [D, TOK], BF16)
    agxnA = dt("agxnA", [NCORES * (D // 2), TOK], BF16, addr_space="Shared")
    agxnB = dt("agxnB", [NCORES * (D // 2), TOK], BF16, addr_space="Shared")
    ag2_in = dt("ag2_in", [1, 16], F32)
    ag2_out = dt("ag2_out", [NCORES, 16], F32, addr_space="Shared")
    qt_dram = dt("qt_dram", [NQH * 128, S], BF16)
    ago_in = dt("ago_in", [NQH * 128, S], BF16)
    ago_out = dt("ago_out", [NQH * NCORES * 128, S], BF16, addr_space="Shared")
    out = dt("out", [TOK, D], F32, kind="ExternalOutput")

    dbg_t = {}
    for name, shape, dty in [
        ("dbg_xnt", [D, TOK], BF16),
        ("dbg_qt", [NQH * 128, S], BF16),
        ("dbg_kt", [128, S], BF16),
        ("dbg_v", [128, NKT * 128], BF16),
        ("dbg_ago", [NCORES * NQH * 128, S], BF16),
        ("dbg_ag2", [NCORES, 16], F32),
        ("dbg_glog", [128, NT * 3], F32),
        ("dbg_wgt", [128, NT * 3], F32),
    ]:
        if name in dbg:
            dbg_t[name] = dt(name, shape, dty, kind="ExternalOutput")

    def sball(name, shape, dty):
        return es.enter_context(nc.sbuf_tensor(name, shape, dty))

    TE, VE, SE, GE, SY = nc.tensor, nc.vector, nc.scalar, nc.gpsimd, nc.sync


    sem = {n: p.sem(n) for n in [
        "rG", "rS", "x0", "x1", "v1", "a1", "t1", "agw", "cc",
        "agx0", "agx1", "cs0", "cs1", "t2", "v2", "a2", "sqt",
        "qtl0", "qtl1", "qtl2", "qtl3", "t3s", "a3e", "v3m", "t3p", "t3n",
        "a3n", "v3n", "v3a", "t3d",
        "sago0", "sago1",
        "gs0", "gs1", "gs2", "gs3", "bb", "bbg", "t4", "a4", "v4", "sxe", "vacc", "gacc",
        "os0", "os1", "dbgs", "vself", "aself"]}

    def vedge(ins):
        p.inc(sem["vself"], ins, 1)
        p.wge(VE, sem["vself"], sem["vself"].n)
        return ins

    def aedge(ins):
        p.inc(sem["aself"], ins, 1)
        p.wge(SE, sem["aself"], sem["aself"].n)
        return ins

    g_ident = sball("g_ident", [128, 128], BF16)
    g_identf = sball("g_identf", [128, 128], F32)
    g_ones_col = sball("g_ones_col", [128, 1], BF16)
    g_ones_colf = sball("g_ones_colf", [128, 1], F32)
    g_ones_row = sball("g_ones_row", [1, 128], F32)
    g_ones8 = sball("g_ones8", [8, 1], F32)
    g_gw = sball("g_gw", [128, NDC * 3], BF16)
    g_b1 = sball("g_b1", [128, 96], F32)
    g_gwlast = sball("g_gwlast", [1, 3], F32)
    g_gb = sball("g_gb", [1, 3], F32)
    g_vmask = sball("g_vmask", [128, NT], F32)
    g_is0 = sball("g_is0", [1, 1], F32)
    g_glog = sball("g_glog", [128, NT * 3], F32)
    g_ag2 = sball("g_ag2", [1, 16], F32)
    g_glog0 = sball("g_glog0", [1, 3], F32)
    g_mod = sball("g_mod", [1, 2], F32)
    g_agoff = sball("g_agoff", [1, 1], I32)

    for dst, src in [
        (g_ident[:], identin[:]), (g_identf[:], identfin[:]),
        (g_gw[:], gwh.rearrange("p c e -> p (c e)")),
        (g_b1[:], b1h[:]), (g_gwlast[:], gwlast[:]), (g_gb[:], gbin[:]),
        (g_vmask[:], vmaskin[:]), (g_is0[:], is0in[:]),
        (g_agoff[:], agoffin[:]),
    ]:
        p.inc(sem["rG"], GE.dma_start(out=dst, in_=src), 16)
    rg_consts = sem["rG"].n
    ins = None
    for buf, val in [(g_ones_col, 1.0), (g_ones_colf, 1.0), (g_ones_row, 1.0),
                     (g_ones8, 1.0), (g_ag2, 0.0)]:
        ins = VE.memset(buf[:], val)
    p.inc(sem["v1"], ins, 1)

    esA = contextlib.ExitStack()

    def sbA(name, shape, dty):
        return esA.enter_context(nc.sbuf_tensor(name, shape, dty))

    x_sb = sbA("x_sb", [128, 2, D], F32)
    xnf = sbA("xnf", [128, 2, D], F32)
    xnbf = sbA("xnbf", [128, 2, D], BF16)
    junk = sbA("junk", [128, D], BF16)
    xnt = sbA("xnt", [128, NDC, NT, 128], BF16)
    st_s1 = sbA("st_s1", [128, NT], F32)
    st_m = sbA("st_m", [128, NT], F32)
    st_ssq = sbA("st_ssq", [128, NT], F32)
    st_m2 = sbA("st_m2", [128, NT], F32)
    st_var = sbA("st_var", [128, NT], F32)
    st_lnv = sbA("st_lnv", [128, NT], F32)
    st_rstd = sbA("st_rstd", [128, NT], F32)
    st_sabs = sbA("st_sabs", [128, NT], F32)
    vm2 = sbA("vm2", [128, 2], F32)
    g_xn0 = sbA("g_xn0", [1, D], F32)
    esPP = contextlib.ExitStack()
    psA = esPP.enter_context(nc.psum_tensor("psA", [128, 1024], BF16))
    psB = esPP.enter_context(nc.psum_tensor("psB", [128, 1024], BF16))
    gate_ps = esPP.enter_context(nc.psum_tensor("gate_ps", [128, 3], F32))
    mod_ps = esPP.enter_context(nc.psum_tensor("mod_ps", [1, 2], F32))

    xsem = [sem["x0"], sem["x1"]]
    v_xnf = [0] * NT
    v_evac = [[0] * 4 for _ in range(NT)]
    a_sq = [0] * NT
    a_abs = [0] * NT
    t_tr = [[0] * 4 for _ in range(NT)]
    t_gate = [0] * NT
    t_mod = 0
    a_gate = 0

    for tt in range(NT):
        b = tt % 2
        sl = slice(tt, tt + 1)
        if tt >= 2:
            p.wge(SY, sem["a1"], a_abs[tt - 2])
            p.wge(SY, sem["v1"], v_xnf[tt - 2])
        p.inc(xsem[b], SY.dma_start(out=x_sb[:, b, :],
                                    in_=x[tt * 128:(tt + 1) * 128, :]), 16)
        xw = 16 * (tt // 2 + 1)
        p.wge(SE, xsem[b], xw)
        p.inc(sem["a1"], SE.activation(junk[:], x_sb[:, b, :], AF.Square,
                                       accum_out=st_ssq[:, sl]), 1)
        a_sq[tt] = sem["a1"].n
        p.wge(VE, xsem[b], xw)
        vedge(VE.reduce_sum(st_s1[:, sl], x_sb[:, b, :], axis=AX.X))
        vedge(VE.tensor_scalar_mul(st_m[:, sl], st_s1[:, sl], float(1.0 / D)))
        p.wge(VE, sem["a1"], a_sq[tt])
        vedge(VE.tensor_scalar_mul(st_var[:, sl], st_ssq[:, sl], float(1.0 / D)))
        vedge(VE.tensor_scalar(st_m2[:, sl], st_m[:, sl], st_m[:, tt:tt + 1], None, ALU.mult))
        p.inc(sem["v1"], VE.tensor_sub(st_var[:, sl], st_var[:, sl], st_m2[:, sl]), 1)
        v_var = sem["v1"].n
        p.wge(SE, sem["v1"], v_var)
        aedge(SE.activation(st_lnv[:, sl], st_var[:, sl], AF.Ln, bias=f32(1e-5)))
        p.inc(sem["a1"], SE.activation(st_rstd[:, sl], st_lnv[:, sl], AF.Exp,
                                       scale=f32(-0.5)), 1)
        a_rstd = sem["a1"].n
        p.wge(VE, sem["a1"], a_rstd)
        if tt >= 2:
            p.wge(VE, sem["a1"], a_abs[tt - 2])
            p.wge(VE, sem["t1"], t_tr[tt - 2][3])
        VE.tensor_scalar(xnf[:, b, :], x_sb[:, b, :], st_m[:, tt:tt + 1],
                         st_rstd[:, tt:tt + 1], ALU.subtract, ALU.mult)
        ins = VE.tensor_copy(xnbf[:, b, :], xnf[:, b, :])
        if tt == 0:
            ins = VE.tensor_copy(g_xn0[:], xnf[0:1, 0, :])
        p.inc(sem["v1"], ins, 1)
        v_xnf[tt] = sem["v1"].n
        p.wge(SE, sem["v1"], v_xnf[tt])
        p.inc(sem["a1"], SE.activation(junk[:], xnf[:, b, :], AF.Abs,
                                       accum_out=st_sabs[:, sl]), 1)
        a_abs[tt] = sem["a1"].n
        p.wge(VE, sem["a1"], a_abs[tt])
        p.wge(VE, sem["t1"], t_mod)
        VE.tensor_copy(vm2[:, 0:1], st_sabs[:, sl])
        p.inc(sem["v1"], VE.tensor_tensor(vm2[:, 1:2], st_sabs[:, sl],
                                          g_vmask[:, sl], ALU.mult), 1)
        v_vm2 = sem["v1"].n
        p.wge(TE, sem["rG"], rg_consts)
        p.wge(TE, sem["v1"], v_xnf[tt])
        for j in range(4):
            pbuf = psA if j % 2 == 0 else psB
            gidx = tt * 4 + j
            if gidx >= 2:
                ptt, pj = (gidx - 2) // 4, (gidx - 2) % 4
                p.wge(TE, sem["v1"], v_evac[ptt][pj])
            last = None
            for c in range(8):
                dc = 8 * j + c
                last = TE.transpose(pbuf[:, c * 128:(c + 1) * 128],
                                    xnbf[:, b, dc * 128:(dc + 1) * 128], g_ident[:])
            p.inc(sem["t1"], last, 1)
            t_tr[tt][j] = sem["t1"].n
            p.wge(VE, sem["t1"], t_tr[tt][j])
            p.inc(sem["v1"], VE.tensor_copy(
                xnt[:, 8 * j:8 * j + 8, tt, :],
                pbuf.rearrange("p (c t) -> p c t", c=8)), 1)
            v_evac[tt][j] = sem["v1"].n
        p.wge(TE, sem["v1"], v_evac[tt][3])
        if tt >= 1:
            p.wge(TE, sem["a2"], a_gate)
        lastg = None
        for dc in range(NDC):
            lastg = TE.matmul(gate_ps[:], xnt[:, dc, tt, :],
                              g_gw[:, dc * 3:(dc + 1) * 3],
                              start=(dc == 0), stop=(dc == NDC - 1))
        p.inc(sem["t1"], lastg, 1)
        t_gate[tt] = sem["t1"].n
        p.wge(SE, sem["t1"], t_gate[tt])
        p.inc(sem["a2"], SE.copy(g_glog[:, tt * 3:(tt + 1) * 3], gate_ps[:]), 1)
        a_gate = sem["a2"].n
        p.wge(TE, sem["v1"], v_vm2)
        p.inc(sem["t1"], TE.matmul(mod_ps[:], g_ones_colf[:], vm2[:],
                                   start=(tt == 0), stop=(tt == NT - 1)), 1)
        t_mod = sem["t1"].n
    p.wge(SE, sem["t1"], t_mod)
    p.inc(sem["a1"], SE.copy(g_mod[:], mod_ps[:]), 1)
    a_mod = sem["a1"].n
    a1_p1 = sem["a1"].n
    t1_p1 = sem["t1"].n

    p.wge(SY, sem["v1"], v_evac[NT - 1][3])
    for dcg in range(4):
        p.inc(sem["agw"], SY.dma_start(
            out=ag1_in.rearrange("(c p) t -> p c t", p=128)[:, 8 * dcg:8 * dcg + 8, :],
            in_=xnt[:, 8 * dcg:8 * dcg + 8, :, :].rearrange("p c u t -> p c (u t)")), 16)
    p.wge(GE, sem["agw"], 32)
    p.inc(sem["cc"], GE.collective_compute(
        "AllGather", ALU.bypass, replica_groups=[list(range(NCORES))],
        ins=[ag1_in[0:D // 2, :]], outs=[agxnA[:]]), 1)
    p.wge(GE, sem["agw"], sem["agw"].n)
    p.inc(sem["cc"], GE.collective_compute(
        "AllGather", ALU.bypass, replica_groups=[list(range(NCORES))],
        ins=[ag1_in[D // 2:D, :]], outs=[agxnB[:]]), 1)

    gwt0_sb = sbA("gwt0_sb", [1, D], F32)
    dot_t = sbA("dot_t", [1, D], F32)
    for e in range(3):
        if e > 0:
            p.wge(SY, sem["v1"], sem["v1"].n)
        p.inc(sem["rS"], SY.dma_start(out=gwt0_sb[:], in_=gwt0[0:1, e * D:(e + 1) * D]), 16)
        p.wge(VE, sem["rS"], sem["rS"].n)
        p.wge(VE, sem["v1"], v_xnf[0])
        VE.tensor_tensor(dot_t[:], g_xn0[:], gwt0_sb[:], ALU.mult)
        ins = VE.reduce_sum(g_glog0[0:1, e:e + 1], dot_t[:], axis=AX.X)
        p.inc(sem["v1"], ins, 1)
        p.wge(VE, sem["v1"], sem["v1"].n)
    p.wge(VE, sem["a1"], a_mod)
    VE.tensor_scalar(g_ag2[0:1, 0:3], g_glog0[:], g_is0[0:1, 0:1], None, ALU.mult)
    p.inc(sem["v1"], VE.tensor_copy(g_ag2[0:1, 3:5], g_mod[:]), 1)
    v_ag2 = sem["v1"].n
    p.wge(GE, sem["v1"], v_ag2)
    p.inc(sem["rG"], GE.dma_start(out=ag2_in[:], in_=g_ag2[:]), 16)
    p.wge(GE, sem["rG"], sem["rG"].n)
    p.inc(sem["cc"], GE.collective_compute(
        "AllGather", ALU.bypass, replica_groups=[list(range(NCORES))],
        ins=[ag2_in[:]], outs=[ag2_out[:]]), 1)

    if "dbg_xnt" in dbg_t:
        p.wge(GE, sem["agw"], sem["agw"].n)
        p.inc(sem["dbgs"], GE.dma_start(out=dbg_t["dbg_xnt"][:], in_=ag1_in[:]), 16)
    if "dbg_ag2" in dbg_t:
        p.wge(GE, sem["cc"], 3)
        p.inc(sem["dbgs"], GE.dma_start(out=dbg_t["dbg_ag2"][:], in_=ag2_out[:]), 16)
    if "dbg_glog" in dbg_t:
        gl_tmp = dt("gl_tmp", [128, NT * 3], F32)
        p.wge(GE, sem["a2"], a_gate)
        p.inc(sem["dbgs"], GE.dma_start(out=gl_tmp[:], in_=g_glog[:]), 16)
        p.wge(GE, sem["dbgs"], sem["dbgs"].n)
        p.inc(sem["dbgs"], GE.dma_start(out=dbg_t["dbg_glog"][:], in_=gl_tmp[:]), 16)

    esA.close()
    esPP.close()

    esKV = contextlib.ExitStack()
    kt_sb = esKV.enter_context(nc.sbuf_tensor("kt_sb", [128, S], BF16))
    v_sb = esKV.enter_context(nc.sbuf_tensor("v_sb", [128, NKT, 128], BF16))
    esB = contextlib.ExitStack()

    def sbB(name, shape, dty):
        return esB.enter_context(nc.sbuf_tensor(name, shape, dty))

    wq_sb = sbB("wq_sb", [128, NDC, NQH, 128], BF16)
    wk_sb = sbB("wk_sb", [128, NDC, 128], BF16)
    wv_sb = sbB("wv_sb", [128, NDC, 128], BF16)
    agx = sbB("agx", [128, 3, 16, 1024], BF16)
    csb = sbB("csb", [128, 2, 2, 512], F32)
    stg_n = sbB("stg_n", [128, 2, 512], F32)
    stg_s = sbB("stg_s", [128, 2, 512], F32)
    rt1 = sbB("rt1", [128, 512], F32)
    rt2 = sbB("rt2", [128, 512], F32)
    qtile = sbB("qtile", [128, 2, 512], BF16)
    vt_sb = sbB("vt_sb", [128, 512], BF16)
    esP2 = contextlib.ExitStack()
    qps = [esP2.enter_context(nc.psum_tensor("qps%d" % i, [128, 512], F32))
           for i in range(NQH)]
    kps = esP2.enter_context(nc.psum_tensor("kps", [128, 512], F32))
    vtps = esP2.enter_context(nc.psum_tensor("vtps", [128, 512], F32))
    vtrps = esP2.enter_context(nc.psum_tensor("vtrps", [128, 512], BF16))

    p.wge(GE, sem["agw"], 64)
    p.wge(GE, sem["v1"], v_ag2)
    p.wge(GE, sem["a1"], a1_p1)
    p.wge(GE, sem["a2"], a_gate)
    p.wge(GE, sem["t1"], t1_p1)
    for dst, src in [
        (wq_sb.rearrange("p a b c -> p (a b c)"), wqh.rearrange("p a b c -> p (a b c)")),
        (wk_sb.rearrange("p a b -> p (a b)"), wkh.rearrange("p a b -> p (a b)")),
        (wv_sb.rearrange("p a b -> p (a b)"), wvh.rearrange("p a b -> p (a b)")),
    ]:
        p.inc(sem["rG"], GE.dma_start(out=dst, in_=src), 16)
    rg_w2 = sem["rG"].n

    css = [sem["cs0"], sem["cs1"]]
    t_half = {}
    t_vtr = {}
    a_vt = {}
    a_ropecp = {}
    v_rope = {}
    v_ropeseq = {}
    v_vsb = {}
    s_qtq = {}
    rope_seq = 0
    qseq = 0

    agxn_vs = [t.rearrange("(r c p) t -> r p c t", r=NCORES, p=128)
               for t in (agxnA, agxnB)]

    def emit_agx_load(u):
        if u >= NTC:
            return
        ptc, h = u // 2, u % 2
        bu = u % 3
        if u >= 3:
            i2c = (2 * ((u - 3) // 2) + 1) * 2 + (u - 3) % 2
            p.wge(SY, sem["t2"], t_half[i2c])
            p.wge(GE, sem["t2"], t_half[i2c])
        p.wge(SY, sem["cc"], 1 + h)
        p.wge(GE, sem["cc"], 1 + h)
        p.inc(sem["agx0"], SY.dma_start(
            out=agx[:, bu, 0:8, :], in_=agxn_vs[h][ptc, :, 0:8, :]), 16)
        p.inc(sem["agx1"], GE.dma_start(
            out=agx[:, bu, 8:16, :], in_=agxn_vs[h][ptc, :, 8:16, :]), 16)

    for u in range(3):
        emit_agx_load(u)

    for tc in range(NTC):
        bcs = tc % 2
        if tc == 0:
            p.wge(SE, sem["t1"], t1_p1)
            p.wge(SE, sem["v1"], v_ag2)
            p.wge(SE, sem["agw"], 64)
        if tc >= 2:
            p.wge(SE, sem["v2"], v_rope[(tc - 2, 4)])
        p.inc(css[bcs], SE.dma_start(out=csb[:, bcs, 0, :],
                                     in_=ctabin[:, tc * 512:(tc + 1) * 512]), 16)
        p.inc(css[bcs], SE.dma_start(out=csb[:, bcs, 1, :],
                                     in_=stabin[:, tc * 512:(tc + 1) * 512]), 16)
        p.wge(TE, sem["rG"], rg_w2)
        if tc == 0:
            p.wge(TE, sem["a2"], a_gate)
            p.wge(TE, sem["a1"], a1_p1)
        for h in range(2):
            i2 = tc * 2 + h
            u = (tc // 2) * 2 + h
            p.wge(TE, sem["agx0"], 16 * (u + 1))
            p.wge(TE, sem["agx1"], 16 * (u + 1))
            if tc > 0 and h == 0:
                p.wge(TE, sem["a2"], a_ropecp[(tc - 1, 4)])
                p.wge(TE, sem["a2"], a_vt[tc - 1])
            last = None
            c0 = (tc % 2) * 512
            for dchalf in range(16):
                dc = 16 * h + dchalf
                rhs = agx[:, u % 3, dchalf, c0:c0 + 512]
                for qh in range(NQH):
                    TE.matmul(qps[qh][:], wq_sb[:, dc, qh, :], rhs,
                              start=(dc == 0), stop=(dc == NDC - 1))
                TE.matmul(kps[:], wk_sb[:, dc, :], rhs,
                          start=(dc == 0), stop=(dc == NDC - 1))
                last = TE.matmul(vtps[:], wv_sb[:, dc, :], rhs,
                                 start=(dc == 0), stop=(dc == NDC - 1))
            p.inc(sem["t2"], last, 1)
            t_half[i2] = sem["t2"].n
            if tc % 2 == 1:
                emit_agx_load(u + 3)
        p.wge(SE, sem["t2"], t_half[tc * 2 + 1])
        if tc >= 1:
            p.wge(SE, sem["t2"], t_vtr[tc - 1])
        p.inc(sem["a2"], SE.copy(vt_sb[:], vtps[:]), 1)
        a_vt[tc] = sem["a2"].n
        p.wge(TE, sem["a2"], a_vt[tc])
        if tc >= 1:
            p.wge(TE, sem["v2"], v_vsb[tc - 1])
        lastv = None
        for j in range(4):
            lastv = TE.transpose(vtrps[:, j * 128:(j + 1) * 128],
                                 vt_sb[:, j * 128:(j + 1) * 128], g_ident[:])
        p.inc(sem["t2"], lastv, 1)
        t_vtr[tc] = sem["t2"].n
        p.wge(VE, sem["t2"], t_vtr[tc])
        p.inc(sem["v2"], VE.tensor_copy(
            v_sb[:, tc * 4:tc * 4 + 4, :].rearrange("p a b -> p (a b)"), vtrps[:]), 1)
        v_vsb[tc] = sem["v2"].n
        for head in range(5):
            src_ps = qps[head] if head < 4 else kps
            rb = rope_seq % 2
            p.wge(SE, sem["t2"], t_half[tc * 2 + 1])
            if rope_seq >= 2:
                p.wge(SE, sem["v2"], v_ropeseq[rope_seq - 2])
            SE.copy(stg_n[:, rb, :], src_ps[:])
            SE.copy(stg_s[0:64, rb, :], src_ps[64:128, :])
            p.inc(sem["a2"], SE.copy(stg_s[64:128, rb, :], src_ps[0:64, :]), 1)
            a_ropecp[(tc, head)] = sem["a2"].n
            p.wge(VE, sem["a2"], a_ropecp[(tc, head)])
            p.wge(VE, css[bcs], 32 * (tc // 2 + 1))
            if head < 4:
                qb = qseq % 2
                dst = qtile[:, qb, :]
                if qseq >= 2:
                    p.wge(VE, sem["sqt"], s_qtq[qseq - 2])
            else:
                dst = kt_sb[:, tc * 512:(tc + 1) * 512]
            VE.tensor_tensor(rt1[:], stg_n[:, rb, :], csb[:, bcs, 0, :], ALU.mult)
            VE.tensor_tensor(rt2[:], stg_s[:, rb, :], csb[:, bcs, 1, :], ALU.mult)
            ins = VE.tensor_tensor(dst, rt1[:], rt2[:], ALU.add)
            p.inc(sem["v2"], ins, 1)
            v_rope[(tc, head)] = sem["v2"].n
            v_ropeseq[rope_seq] = sem["v2"].n
            if head < 4:
                p.wge(SY, sem["v2"], v_rope[(tc, head)])
                p.inc(sem["sqt"], SY.dma_start(
                    out=qt_dram[head * 128:(head + 1) * 128, tc * 512:(tc + 1) * 512],
                    in_=qtile[:, qseq % 2, :]), 16)
                s_qtq[qseq] = sem["sqt"].n
                qseq += 1
            rope_seq += 1
    v2_p2 = sem["v2"].n
    a2_p2 = sem["a2"].n
    sqt_total = sem["sqt"].n
    esB.close()
    esP2.close()

    if "dbg_qt" in dbg_t:
        p.wge(GE, sem["sqt"], sqt_total)
        p.inc(sem["dbgs"], GE.dma_start(out=dbg_t["dbg_qt"][:], in_=qt_dram[:]), 16)
    if "dbg_kt" in dbg_t:
        kt_tmp = dt("kt_tmp", [128, S], BF16)
        p.wge(GE, sem["v2"], v2_p2)
        p.inc(sem["dbgs"], GE.dma_start(out=kt_tmp[:], in_=kt_sb[:]), 16)
        p.wge(GE, sem["dbgs"], sem["dbgs"].n)
        p.inc(sem["dbgs"], GE.dma_start(out=dbg_t["dbg_kt"][:], in_=kt_tmp[:]), 16)
    if "dbg_v" in dbg_t:
        v_tmp = dt("v_tmp", [128, NKT * 128], BF16)
        p.wge(GE, sem["v2"], v2_p2)
        p.inc(sem["dbgs"], GE.dma_start(out=v_tmp[:],
                                        in_=v_sb.rearrange("p a b -> p (a b)")), 16)
        p.wge(GE, sem["dbgs"], sem["dbgs"].n)
        p.inc(sem["dbgs"], GE.dma_start(out=dbg_t["dbg_v"][:], in_=v_tmp[:]), 16)

    esC = contextlib.ExitStack()

    def sbC(name, shape, dty):
        return esC.enter_context(nc.sbuf_tensor(name, shape, dty))

    g_mask = sbC("g_mask", [128, 4, 512], BF16)
    qtb = sbC("qtb", [128, 4, 512], BF16)
    pt_sb = sbC("pt_sb", [128, 4, 1024], BF16)
    obuf = sbC("obuf", [128, 2, 512], BF16)
    lntmp = sbC("lntmp", [1, 512], F32)
    rc_sb = sbC("rc_sb", [1, 2, 512], F32)
    rcb_sb = sbC("rcb_sb", [128, 2, 512], F32)
    tmpb3 = sbC("tmpb3", [128, 2, 512], BF16)
    tmpq3 = sbC("tmpq3", [128, 2, 512], BF16)
    esP3 = contextlib.ExitStack()
    stps = [esP3.enter_context(nc.psum_tensor("stps%d" % i, [128, 1024], F32))
            for i in range(2)]
    ops = [esP3.enter_context(nc.psum_tensor("ops%d" % i, [128, 512], F32))
           for i in range(2)]
    rcbps = [esP3.enter_context(nc.psum_tensor("rcbps%d" % i, [128, 512], F32))
             for i in range(2)]

    p.wge(GE, sem["t2"], sem["t2"].n)
    p.wge(GE, sem["v2"], v2_p2)
    p.wge(GE, sem["a2"], a2_p2)
    p.wge(GE, sem["sqt"], sqt_total)
    p.inc(sem["rG"], GE.dma_start(out=g_mask.rearrange("p k q -> p (k q)"),
                                  in_=maskin.rearrange("p k q -> p (k q)")), 16)
    rg_mask = sem["rG"].n

    qtls = [sem["qtl0"], sem["qtl1"], sem["qtl2"], sem["qtl3"]]
    sagos = [sem["sago0"], sem["sago1"]]
    groups = [(tc, qh) for qh in range(NQH) for tc in range(NTC)]
    bi = 0
    t3s_at = {}
    a3e_at = {}
    t3p_at = {}
    grp_last_b = {}
    v3n_at = {}
    a3n_at = {}
    v3a_at = {}
    t3d_at = {}
    t3n_rcb = {}
    pend = []

    def emit_pv(it):
        (bb, j, g0, lb, first, last_it, diag, mi) = it
        if diag:
            p.wge(TE, sem["v3m"], mi)
        else:
            p.wge(TE, sem["a3e"], a3e_at[bb])
        if first:
            p.wge(TE, sem["v3n"], v3n_at.get(j - 2, 0))
        TE.matmul(ops[j % 2][:], v_sb[:, g0, :], pt_sb[:, bb % 4, 0:512],
                  start=first, stop=False)
        ins = TE.matmul(ops[j % 2][:], v_sb[:, g0 + 1, :], pt_sb[:, bb % 4, 512:1024],
                        start=False, stop=last_it)
        if lb % 2 == 1:
            if lb == 1:
                p.wge(TE, sem["a3n"], a3n_at.get(j - 2, 0))
            p.wge(TE, sem["v3a"], v3a_at[bb])
            ins = TE.matmul(rcbps[j % 2][0:1, :], g_ones_col[:],
                            tmpq3[:, (bb // 2) % 2, :],
                            start=(lb == 1), stop=last_it)
        p.inc(sem["t3p"], ins, 1)
        t3p_at[bb] = sem["t3p"].n

    for j, (tc, qh) in enumerate(groups):
        if j >= 4:
            p.wge(SY, sem["t3s"], t3s_at[grp_last_b[j - 4]])
        if j == 0:
            p.wge(SY, sem["sqt"], sqt_total)
        p.inc(qtls[j % 4], SY.dma_start(
            out=qtb[:, j % 4, :],
            in_=qt_dram[qh * 128:(qh + 1) * 128, tc * 512:(tc + 1) * 512]), 16)
        nb = 2 * (tc + 1)
        for lb in range(nb):
            g0 = 2 * lb
            first, last_it = lb == 0, lb == nb - 1
            diag = g0 // 4 == tc
            if bi == 0:
                p.wge(TE, sem["v2"], v2_p2)
                p.wge(TE, sem["a2"], a2_p2)
                p.wge(TE, sem["rG"], rg_mask)
            p.wge(TE, qtls[j % 4], 16 * (j // 4 + 1))
            p.wge(TE, sem["a3e"], a3e_at.get(bi - 2, 0))
            TE.matmul(stps[bi % 2][:, 0:512], kt_sb[:, g0 * 128:(g0 + 1) * 128],
                      qtb[:, j % 4, :], start=True, stop=True)
            ins = TE.matmul(stps[bi % 2][:, 512:1024],
                            kt_sb[:, (g0 + 1) * 128:(g0 + 2) * 128],
                            qtb[:, j % 4, :], start=True, stop=True)
            p.inc(sem["t3s"], ins, 1)
            t3s_at[bi] = sem["t3s"].n
            p.wge(SE, sem["t3s"], t3s_at[bi])
            p.wge(SE, sem["t3p"], t3p_at.get(bi - 4, 0))
            p.wge(SE, sem["v3a"], v3a_at.get(bi - 4, 0))
            ins = SE.activation(pt_sb[:, bi % 4, :], stps[bi % 2][:], AF.Exp,
                                scale=ISQ, bias=NEG_B)
            p.inc(sem["a3e"], ins, 1)
            a3e_at[bi] = sem["a3e"].n
            mi = 0
            p.wge(VE, sem["a3e"], a3e_at[bi])
            if diag:
                VE.tensor_tensor(pt_sb[:, bi % 4, 0:512], pt_sb[:, bi % 4, 0:512],
                                 g_mask[:, (g0 % 4), :], ALU.mult)
                ins = VE.tensor_tensor(pt_sb[:, bi % 4, 512:1024],
                                       pt_sb[:, bi % 4, 512:1024],
                                       g_mask[:, (g0 % 4) + 1, :], ALU.mult)
                p.inc(sem["v3m"], ins, 1)
                mi = sem["v3m"].n
            ins = VE.tensor_tensor(tmpb3[:, bi % 2, :], pt_sb[:, bi % 4, 0:512],
                                   pt_sb[:, bi % 4, 512:1024], ALU.add)
            p.inc(sem["v3a"], ins, 1)
            if lb % 2 == 1:
                p.wge(VE, sem["t3p"], t3p_at.get(bi - 4, 0))
                ins = VE.tensor_tensor(tmpq3[:, (bi // 2) % 2, :], tmpb3[:, 0, :],
                                       tmpb3[:, 1, :], ALU.add)
                p.inc(sem["v3a"], ins, 1)
            v3a_at[bi] = sem["v3a"].n
            if pend:
                emit_pv(pend.pop())
            pend.append((bi, j, g0, lb, first, last_it, diag, mi))
            bi += 1
        grp_last_b[j] = bi - 1
        emit_pv(pend.pop())
        p.wge(SE, sem["t3p"], t3p_at[grp_last_b[j]])
        SE.activation(lntmp[:], rcbps[j % 2][0:1, :], AF.Ln)
        ins = SE.activation(rc_sb[0:1, j % 2, :], lntmp[0:1, :], AF.Exp,
                            scale=f32(-1.0))
        p.inc(sem["a3n"], ins, 1)
        a3n_exp = sem["a3n"].n
        p.wge(TE, sem["a3n"], a3n_exp)
        ins = TE.matmul(rcbps[j % 2][:], g_ones_row[:], rc_sb[0:1, j % 2, :],
                        start=True, stop=True)
        p.inc(sem["t3n"], ins, 1)
        t3n_rcb[j] = sem["t3n"].n
        p.wge(SE, sem["t3n"], t3n_rcb[j])
        p.inc(sem["a3n"], SE.copy(rcb_sb[:, j % 2, :], rcbps[j % 2][:]), 1)
        a3n_at[j] = sem["a3n"].n
        p.wge(VE, sem["a3n"], a3n_at[j])
        p.wge(VE, sagos[j % 2], 16 * (j // 2))
        p.wge(VE, sem["t3p"], t3p_at[grp_last_b[j]])
        ins = VE.tensor_tensor(obuf[:, j % 2, :], ops[j % 2][:],
                               rcb_sb[:, j % 2, :], ALU.mult)
        p.inc(sem["v3n"], ins, 1)
        v3n_at[j] = sem["v3n"].n
        p.wge(SY, sem["v3n"], v3n_at[j])
        p.inc(sagos[j % 2], SY.dma_start(
            out=ago_in[qh * 128:(qh + 1) * 128, tc * 512:(tc + 1) * 512],
            in_=obuf[:, j % 2, :]), 16)
        if tc == NTC - 1:
            p.wge(GE, sagos[0], sagos[0].n)
            p.wge(GE, sagos[1], sagos[1].n)
            p.inc(sem["cc"], GE.collective_compute(
                "AllGather", ALU.bypass, replica_groups=[list(range(NCORES))],
                ins=[ago_in[qh * 128:(qh + 1) * 128, :]],
                outs=[ago_out[qh * NCORES * 128:(qh + 1) * NCORES * 128, :]]), 1)
    if "dbg_ago" in dbg_t:
        p.wge(GE, sem["cc"], 7)
        for qh in range(NQH):
            p.wge(GE, sem["dbgs"], sem["dbgs"].n)
            p.inc(sem["dbgs"], GE.dma_start(
                out=dbg_t["dbg_ago"].rearrange("(r h p) t -> h (r p) t", h=NQH, p=128)[qh],
                in_=ago_out.rearrange("(h rp) t -> h rp t", h=NQH)[qh]), 16)
    esC.close()
    esP3.close()
    esKV.close()

    esD = contextlib.ExitStack()

    def sbD(name, shape, dty):
        return esD.enter_context(nc.sbuf_tensor(name, shape, dty))

    esPG = contextlib.ExitStack()
    esPGa = contextlib.ExitStack()
    agg_ps = esPGa.enter_context(nc.psum_tensor("agg_ps", [1, 16], F32))
    g9_ps = esPGa.enter_context(nc.psum_tensor("g9_ps", [128, 9], F32))
    wt_ps = esPGa.enter_context(nc.psum_tensor("wt_ps", [1, 384], F32))

    ag2o_sb = sbD("ag2o_sb", [8, 16], F32)
    agg_sb = sbD("agg_sb", [1, 16], F32)
    tiny = sbD("tiny", [1, 8], F32)
    rvec = sbD("rvec", [1, 3], F32)
    glog0f = sbD("glog0f", [1, 3], F32)
    imax = sbD("imax", [1, 3], F32)
    imin_ = sbD("imin_", [1, 3], F32)
    imed = sbD("imed", [1, 3], F32)
    rb_sb = sbD("rb_sb", [128, 9], F32)
    t3a = sbD("t3a", [128, 3], F32)
    e3b = sbD("e3b", [128, 3], F32)
    s3 = sbD("s3", [128, 1], F32)
    nmx = sbD("nmx", [128, 1], F32)
    w0v = sbD("w0v", [128, 1], F32)
    w1v = sbD("w1v", [128, 1], F32)
    sc1 = sbD("sc1", [128, 1], F32)
    sc2 = sbD("sc2", [128, 1], F32)
    wgt_sb = sbD("wgt_sb", [128, NT * 3], F32)
    wgtT = [sbD("wgtT%d" % e, [1, TOK], F32) for e in range(3)]
    wgtTb = [sbD("wgtTb%d" % e, [1, TOK], BF16) for e in range(3)]
    wgtb = [sbD("wgtb%d" % e, [128, 512], F32) for e in range(3)]

    p.wge(GE, sem["cc"], 3)
    p.wge(GE, sem["v3n"], 64)
    p.wge(GE, sagos[0], sagos[0].n)
    p.wge(GE, sagos[1], sagos[1].n)
    p.inc(sem["rG"], GE.dma_start(out=ag2o_sb[:], in_=ag2_out[:]), 16)
    p.wge(TE, sem["rG"], sem["rG"].n)
    p.inc(sem["t4"], TE.matmul(agg_ps[:], g_ones8[:], ag2o_sb[:],
                               start=True, stop=True), 1)
    p.wge(SE, sem["t4"], sem["t4"].n)
    p.inc(sem["a4"], SE.copy(agg_sb[:], agg_ps[:]), 1)
    p.wge(VE, sem["a4"], sem["a4"].n)
    vedge(VE.tensor_tensor(tiny[0:1, 0:1], agg_sb[0:1, 3:4], agg_sb[0:1, 4:5], ALU.subtract))
    vedge(VE.tensor_scalar_mul(tiny[0:1, 1:2], agg_sb[0:1, 4:5], float(1.0 / (VIS * D))))
    vedge(VE.tensor_scalar_mul(tiny[0:1, 2:3], tiny[0:1, 0:1], float(1.0 / ((S - VIS) * D))))
    vedge(VE.tensor_scalar_add(tiny[0:1, 3:4], tiny[0:1, 1:2], float(1e-8)))
    vedge(VE.reciprocal(tiny[0:1, 4:5], tiny[0:1, 3:4]))
    vedge(VE.tensor_tensor(tiny[0:1, 5:6], tiny[0:1, 2:3], tiny[0:1, 4:5], ALU.mult))
    vedge(VE.tensor_scalar(rvec[:], g_gwlast[:], tiny[0:1, 5:6], None, ALU.mult))
    vedge(VE.tensor_tensor(rvec[:], rvec[:], g_gb[:], ALU.add))
    vedge(VE.tensor_tensor(glog0f[:], agg_sb[0:1, 0:3], rvec[:], ALU.add))
    vedge(VE.reduce_max(tiny[0:1, 6:7], glog0f[:], axis=AX.X))
    vedge(VE.tensor_reduce(tiny[0:1, 7:8], glog0f[:], axis=AX.X, op=ALU.min))
    vedge(VE.tensor_scalar(imax[:], glog0f[:], tiny[0:1, 6:7], None, ALU.is_equal))
    vedge(VE.tensor_scalar(imin_[:], glog0f[:], tiny[0:1, 7:8], None, ALU.is_equal))
    vedge(VE.tensor_tensor(imed[:], imax[:], imin_[:], ALU.add))
    ins = VE.tensor_scalar(imed[:], imed[:], -1.0, 1.0, ALU.mult, ALU.add)
    p.inc(sem["v4"], ins, 1)
    p.wge(VE, sem["v4"], sem["v4"].n)
    p.wge(TE, sem["v4"], sem["v4"].n)
    TE.matmul(g9_ps[:, 0:3], g_ones_row[:], rvec[:], start=True, stop=True)
    TE.matmul(g9_ps[:, 3:6], g_ones_row[:], imax[:], start=True, stop=True)
    p.inc(sem["t4"], TE.matmul(g9_ps[:, 6:9], g_ones_row[:], imed[:],
                               start=True, stop=True), 1)
    p.wge(SE, sem["t4"], sem["t4"].n)
    p.inc(sem["a4"], SE.copy(rb_sb[:], g9_ps[:]), 1)
    a4_rb = sem["a4"].n
    a4_wT = 0
    for tt in range(NT):
        sl3 = slice(tt * 3, tt * 3 + 3)
        p.wge(VE, sem["a4"], a4_rb)
        vedge(VE.tensor_tensor(t3a[:], g_glog[:, sl3], rb_sb[:, 0:3], ALU.add))
        vedge(VE.reduce_max(nmx[:], t3a[:], axis=AX.X))
        ins = VE.tensor_scalar_mul(nmx[:], nmx[:], -1.0)
        p.inc(sem["v4"], ins, 1)
        p.wge(VE, sem["v4"], sem["v4"].n)
        p.wge(SE, sem["v4"], sem["v4"].n)
        p.inc(sem["a4"], SE.activation(e3b[:], t3a[:], AF.Exp, bias=nmx[:, 0:1],
                                       accum_out=s3[:]), 1)
        p.wge(VE, sem["a4"], sem["a4"].n)
        vedge(VE.reciprocal(sc1[:], s3[:]))
        vedge(VE.reduce_max(sc2[:], e3b[:], axis=AX.X))
        vedge(VE.tensor_tensor(w0v[:], sc2[:], sc1[:], ALU.mult))
        vedge(VE.tensor_reduce(w1v[:], e3b[:], axis=AX.X, op=ALU.min))
        vedge(VE.tensor_tensor(s3[:], s3[:], sc2[:], ALU.subtract))
        vedge(VE.tensor_tensor(s3[:], s3[:], w1v[:], ALU.subtract))
        vedge(VE.tensor_tensor(w1v[:], s3[:], sc1[:], ALU.mult))
        vedge(VE.tensor_tensor(sc2[:], w0v[:], w1v[:], ALU.add))
        vedge(VE.tensor_scalar_add(sc2[:], sc2[:], float(1e-8)))
        vedge(VE.reciprocal(sc1[:], sc2[:]))
        vedge(VE.tensor_tensor(w0v[:], w0v[:], sc1[:], ALU.mult))
        vedge(VE.tensor_tensor(w1v[:], w1v[:], sc1[:], ALU.mult))
        vedge(VE.tensor_scalar(t3a[:], rb_sb[:, 3:6], w0v[:, 0:1], None, ALU.mult))
        vedge(VE.tensor_scalar(e3b[:], rb_sb[:, 6:9], w1v[:, 0:1], None, ALU.mult))
        ins = VE.tensor_tensor(wgt_sb[:, sl3], t3a[:], e3b[:], ALU.add)
        p.inc(sem["v4"], ins, 1)
        p.wge(VE, sem["v4"], sem["v4"].n)
        v4_wgt = sem["v4"].n
        p.wge(TE, sem["v4"], v4_wgt)
        p.wge(TE, sem["a4"], a4_wT)
        lastt = None
        for e in range(3):
            lastt = TE.transpose(wt_ps[0:1, e * 128:(e + 1) * 128],
                                 wgt_sb[:, tt * 3 + e:tt * 3 + e + 1], g_identf[:])
        p.inc(sem["t4"], lastt, 1)
        p.wge(SE, sem["t4"], sem["t4"].n)
        ins = None
        for e in range(3):
            ins = SE.copy(wgtT[e][0:1, tt * 128:(tt + 1) * 128],
                          wt_ps[0:1, e * 128:(e + 1) * 128])
        p.inc(sem["a4"], ins, 1)
        a4_wT = sem["a4"].n
    p.wge(VE, sem["a4"], a4_wT)
    ins = None
    for e in range(3):
        ins = VE.tensor_copy(wgtTb[e][:], wgtT[e][:])
    p.inc(sem["v4"], ins, 1)
    v4_wTb = sem["v4"].n
    if "dbg_wgt" in dbg_t:
        wg_tmp = dt("wg_tmp", [128, NT * 3], F32)
        p.wge(GE, sem["v4"], v4_wTb)
        p.inc(sem["dbgs"], GE.dma_start(out=wg_tmp[:], in_=wgt_sb[:]), 16)
        p.wge(GE, sem["dbgs"], sem["dbgs"].n)
        p.inc(sem["dbgs"], GE.dma_start(out=dbg_t["dbg_wgt"][:], in_=wg_tmp[:]), 16)

    acts = sbD("acts", [128, 3, 32, 512], BF16)
    bigbuf = sbD("bigbuf", [128, 32, 512], BF16)
    slab = sbD("slab", [128, 4, 32, 128], BF16)
    b2sl = sbD("b2sl", [1, 2, 3, 128], BF16)
    xsl = sbD("xsl", [128, 4, 512], F32)
    ostg = sbD("ostg", [128, 2, 512], F32)
    stge = sbD("stge", [128, 2, 512], F32)
    esPGa.close()
    mm_ps = [esPG.enter_context(nc.psum_tensor("mm_ps%d" % i, [128, 512], F32))
             for i in range(2)]
    pst = [esPG.enter_context(nc.psum_tensor("pst%d" % i, [128, 512], F32))
           for i in range(4)]
    wb_ps = pst[0]

    gss = [sem["gs0"], sem["gs1"], sem["gs2"], sem["gs3"]]
    slab_idx = 0
    slab_cons = {}
    t4_w1 = {}
    a4_act = {}
    a4_ev = {}
    t4_tp = {}
    b2_cons = {}
    v4_dcg = {}
    xsl_load = {}
    os_seq = 0
    oss = [sem["os0"], sem["os1"]]
    a4_wb = 0
    t4_dout_last = 0

    ago_rbase = GE.alloc_register("ago_base")
    ago_rtmp = GE.alloc_register("ago_tmp")
    p.wge(GE, sem["rG"], rg_consts)
    GE.reg_load(ago_rbase, g_agoff[0:1, 0:1])

    for qc in range(2):
        if qc == 0:
            p.wge(SY, sem["v3n"], 64)
            p.wge(SY, sagos[0], sagos[0].n)
            p.wge(SY, sagos[1], sagos[1].n)
        if qc > 0:
            p.wge(SY, sem["t4"], t4_dout_last)
        for dcg8 in range(8):
            p.inc(sem["bb"], SY.dma_start(
                out=bigbuf[:, 4 * dcg8:4 * dcg8 + 4, :],
                in_=ag1_in.rearrange("(c p) t -> p c t", p=128)[
                    :, 4 * dcg8:4 * dcg8 + 4, qc * 512:(qc + 1) * 512]), 16)
        bb_xn = sem["bb"].n
        for e in range(3):
            fn = [AF.Tanh, AF.Gelu, AF.Silu][e]
            for hc in range(32):
                si = slab_idx
                sb4 = si % 4
                if si >= 4:
                    p.wge(GE, sem["t4"], slab_cons[si - 4])
                p.inc(gss[sb4], GE.dma_start(
                    out=slab[:, sb4].rearrange("p a b -> p (a b)"),
                    in_=w1s[e, hc].rearrange("p a b -> p (a b)")), 16)
                slab_idx += 1
                gidx = qc * 96 + e * 32 + hc
                p.wge(TE, gss[sb4], 16 * (si // 4 + 1))
                if gidx == 0:
                    p.wge(TE, sem["a4"], a4_wT)
                p.wge(TE, sem["bb"], bb_xn)
                if gidx >= 2:
                    p.wge(TE, sem["a4"], a4_act.get(gidx - 2, 0))
                if qc > 0 and gidx < 96 + 2:
                    p.wge(TE, sem["a4"], a4_ev.get((qc - 1, 30 + gidx - 96), 0))
                last = None
                for dc in range(32):
                    last = TE.matmul(mm_ps[gidx % 2][:], slab[:, sb4, dc, :],
                                     bigbuf[:, dc, :], start=(dc == 0), stop=(dc == 31))
                p.inc(sem["t4"], last, 1)
                slab_cons[si] = sem["t4"].n
                t4_w1[gidx] = sem["t4"].n
                p.wge(SE, sem["t4"], t4_w1[gidx])
                ins = SE.activation(acts[:, e, hc, :], mm_ps[gidx % 2][:], fn,
                                    bias=g_b1[:, e * 32 + hc:e * 32 + hc + 1])
                p.inc(sem["a4"], ins, 1)
                a4_act[gidx] = sem["a4"].n
        p.wge(TE, sem["v4"], v4_wTb)
        if qc > 0:
            p.wge(TE, sem["v4"], v4_dcg.get((qc - 1, 7), 0))
        for e in range(3):
            p.wge(TE, sem["a4"], a4_wb)
            p.inc(sem["t4"], TE.matmul(wb_ps[:], g_ones_row[:],
                                       wgtT[e][0:1, qc * 512:(qc + 1) * 512],
                                       start=True, stop=True), 1)
            p.wge(SE, sem["t4"], sem["t4"].n)
            p.inc(sem["a4"], SE.copy(wgtb[e][:], wb_ps[:]), 1)
            a4_wb = sem["a4"].n
        p.wge(VE, sem["a4"], a4_wb)
        p.wge(VE, sem["a4"], a4_act[qc * 96 + 95])
        ins = None
        for e in range(3):
            for hc in range(32):
                ins = VE.tensor_tensor(acts[:, e, hc, :], acts[:, e, hc, :],
                                       wgtb[e][:], ALU.mult)
        p.inc(sem["v4"], ins, 1)
        v4_presc = sem["v4"].n
        p.wge(GE, sem["t4"], t4_w1[qc * 96 + 95])
        p.wge(GE, sem["cc"], 7)
        for rr in range(8):
            static_off = rr * 128 * S + qc * 512
            GE.reg_mov(ago_rtmp, static_off)
            GE.reg_add(ago_rtmp, ago_rtmp, ago_rbase)
            src = bass.AP(ago_out, ago_rtmp,
                          [[S, 128], [NCORES * 128 * S, 4], [1, 512]])
            p.inc(sem["bbg"], GE.dma_start(
                out=bigbuf.rearrange("p (r q) t -> p q r t", q=4)[:, :, rr, :]
                if False else
                bass.AP(bigbuf, rr * 4 * 512,
                        [[32 * 512, 128], [512, 4], [1, 512]]),
                in_=src), 16)
        bb_ago = sem["bbg"].n
        for dc in range(32):
            b2p = dc % 2
            p.wge(GE, sem["t4"], b2_cons.get((qc, dc - 2), 0))
            p.inc(sem["rG"], GE.dma_start(
                out=b2sl[0:1, b2p],
                in_=b2g.rearrange("o (e q) -> o e q", e=3)[
                    0:1, :, dc * 128:(dc + 1) * 128]), 16)
            rg_b2 = sem["rG"].n
            srcs = [wos[dc]] + [w2s[e, dc] for e in range(3)]
            sis = []
            for src_w in srcs:
                si = slab_idx
                sb4 = si % 4
                if si >= 4:
                    p.wge(GE, sem["t4"], slab_cons[si - 4])
                p.inc(gss[sb4], GE.dma_start(
                    out=slab[:, sb4].rearrange("p a b -> p (a b)"),
                    in_=src_w.rearrange("p a b -> p (a b)")), 16)
                sis.append(si)
                slab_idx += 1
            acc = mm_ps[dc % 2]
            if dc >= 2:
                p.wge(TE, sem["a4"], a4_ev.get((qc, dc - 2), 0))
            else:
                p.wge(TE, sem["a4"], a4_act[qc * 96 + 94 + (dc % 2)])
            for k4, si in enumerate(sis):
                sb4 = si % 4
                p.wge(TE, gss[sb4], 16 * (si // 4 + 1))
                if k4 == 0:
                    p.wge(TE, sem["bbg"], bb_ago)
                else:
                    p.wge(TE, sem["v4"], v4_presc)
                last = None
                for c in range(32):
                    rhs = bigbuf[:, c, :] if k4 == 0 else acts[:, k4 - 1, c, :]
                    last = TE.matmul(acc[:], slab[:, sb4, c, :], rhs,
                                     start=(k4 == 0 and c == 0), stop=False)
                p.inc(sem["t4"], last, 1)
                slab_cons[si] = sem["t4"].n
            p.wge(TE, sem["rG"], rg_b2)
            ins = None
            for e in range(3):
                ins = TE.matmul(acc[:], b2sl[0:1, b2p, e, :],
                                wgtTb[e][0:1, qc * 512:(qc + 1) * 512],
                                start=False, stop=(e == 2))
            p.inc(sem["t4"], ins, 1)
            b2_cons[(qc, dc)] = sem["t4"].n
            t4_mm = sem["t4"].n
            p.wge(SE, sem["t4"], t4_mm)
            p.wge(SE, sem["t4"], t4_tp.get((qc, dc - 2), 0))
            p.inc(sem["a4"], SE.copy(stge[:, dc % 2, :], acc[:]), 1)
            a4_ev[(qc, dc)] = sem["a4"].n
            dcg, dslot = dc // 4, dc % 4
            p.wge(TE, sem["a4"], a4_ev[(qc, dc)])
            if dc >= 4:
                p.wge(TE, sem["v4"], v4_dcg.get((qc, dcg - 1), 0))
            elif qc > 0:
                p.wge(TE, sem["v4"], v4_dcg.get((qc - 1, 7), 0))
            lastt = None
            for tt4 in range(4):
                lastt = TE.transpose(pst[tt4][:, dslot * 128:(dslot + 1) * 128],
                                     stge[:, dc % 2, tt4 * 128:(tt4 + 1) * 128],
                                     g_identf[:])
            p.inc(sem["t4"], lastt, 1)
            t4_tp[(qc, dc)] = sem["t4"].n
            if dslot == 0:
                prev = v4_dcg.get((qc, dcg - 1), v4_dcg.get((qc - 1, 7), 0))
                p.wge(SY, sem["v4"], prev)
                for tt4 in range(4):
                    p.inc(sem["sxe"], SY.dma_start(
                        out=xsl[:, tt4, :],
                        in_=x[qc * 512 + tt4 * 128:qc * 512 + (tt4 + 1) * 128,
                              dcg * 512:(dcg + 1) * 512]), 16)
                xsl_load[(qc, dcg)] = sem["sxe"].n
            if dslot == 3:
                p.wge(VE, sem["t4"], t4_tp[(qc, dc)])
                p.wge(VE, sem["sxe"], xsl_load[(qc, dcg)])
                for tt4 in range(4):
                    op = os_seq % 2
                    p.wge(VE, oss[op], 16 * (os_seq // 2))
                    ins = VE.tensor_tensor(ostg[:, op, :], pst[tt4][:],
                                           xsl[:, tt4, :], ALU.add)
                    p.inc(sem["v4"], ins, 1)
                    p.wge(SY, sem["v4"], sem["v4"].n)
                    p.inc(oss[op], SY.dma_start(
                        out=out[qc * 512 + tt4 * 128:qc * 512 + (tt4 + 1) * 128,
                                dcg * 512:(dcg + 1) * 512],
                        in_=ostg[:, op, :]), 16)
                    os_seq += 1
                v4_dcg[(qc, dcg)] = sem["v4"].n
        t4_dout_last = sem["t4"].n
    p.wge(SY, oss[0], oss[0].n)
    p.wge(SY, oss[1], oss[1].n)
    if sem["dbgs"].n:
        p.wge(SY, sem["dbgs"], sem["dbgs"].n)

    nc._es = es
    return nc, dbg_t


_PERM = np.concatenate([np.arange(0, 128, 2), np.arange(1, 128, 2)])


def _prep_inputs(L0_attn_x, L0_attn_wq, L0_attn_wk, L0_attn_wv, L0_attn_wo,
                 L0_ffn_gate_w, L0_ffn_gate_b, L0_ffn_w1, L0_ffn_b1,
                 L0_ffn_w2, L0_ffn_b2):
    f = np.float32
    x2 = np.asarray(L0_attn_x, f).reshape(S, D)
    wq = np.asarray(L0_attn_wq, f)
    wk = np.asarray(L0_attn_wk, f)
    wv = np.asarray(L0_attn_wv, f)
    wo = np.asarray(L0_attn_wo, f)
    gw = np.asarray(L0_ffn_gate_w, f)
    gb = np.asarray(L0_ffn_gate_b, f)
    w1 = np.asarray(L0_ffn_w1, f)
    b1 = np.asarray(L0_ffn_b1, f)
    w2 = np.asarray(L0_ffn_w2, f)
    b2 = np.asarray(L0_ffn_b2, f)

    wqp = wq.reshape(D, 32, 128)[:, :, _PERM].reshape(32, 128, 32, 128)
    wq_h = np.ascontiguousarray(wqp.transpose(1, 0, 2, 3))
    wkp = wk.reshape(D, 8, 128)[:, :, _PERM].reshape(32, 128, 8, 128)
    wk_h = np.ascontiguousarray(wkp.transpose(1, 0, 2, 3))
    wv_h = np.ascontiguousarray(wv.reshape(32, 128, 8, 128).transpose(1, 0, 2, 3))
    w1s = np.ascontiguousarray(w1.reshape(3, 32, 128, 32, 128).transpose(0, 3, 2, 1, 4))
    w2s = np.ascontiguousarray(w2.reshape(3, 32, 128, 32, 128).transpose(0, 3, 2, 1, 4))
    wos = np.ascontiguousarray(wo.reshape(32, 128, 32, 128).transpose(2, 1, 0, 3))
    gwh = np.ascontiguousarray(gw[:D].reshape(32, 128, 3).transpose(1, 0, 2))
    gwt0 = np.ascontiguousarray(gw[:D].T.reshape(1, 3 * D))
    gwlast = np.ascontiguousarray(gw[D:D + 1, :])
    gbin = np.ascontiguousarray(gb.reshape(1, 3))
    b1h = np.ascontiguousarray(b1.reshape(3, 32, 128).transpose(2, 0, 1).reshape(128, 96))
    b2g = np.ascontiguousarray(b2.reshape(1, 3 * D))

    i64 = np.arange(64)
    freqs = (10000.0 ** (-(2.0 * i64) / 128.0)).astype(np.float64)
    ang = np.outer(np.arange(S, dtype=np.float64), freqs)
    c = np.cos(ang).T.astype(f)
    s = np.sin(ang).T.astype(f)
    ctab = np.ascontiguousarray(np.concatenate([c, c], 0))
    stab = np.ascontiguousarray(np.concatenate([-s, s], 0))

    pp = np.arange(128)[:, None]
    jj = np.arange(512)[None, :]
    maskh = np.stack([(kt * 128 + pp <= jj) for kt in range(4)], axis=1)
    maskh = np.ascontiguousarray(maskh.astype(ml_dtypes.bfloat16))
    ident = np.ascontiguousarray(np.eye(128, dtype=ml_dtypes.bfloat16))
    identf = np.ascontiguousarray(np.eye(128, dtype=f))

    in_maps = []
    for r in range(NCORES):
        tok0 = r * TOK
        vm = ((tok0 + np.arange(NT)[None, :] * 128 + np.arange(128)[:, None])
              < VIS).astype(f)
        in_maps.append({
            "x": np.ascontiguousarray(x2[tok0:tok0 + TOK]),
            "wqh": np.ascontiguousarray(wq_h[:, :, 4 * r:4 * r + 4, :]),
            "wkh": np.ascontiguousarray(wk_h[:, :, r, :]),
            "wvh": np.ascontiguousarray(wv_h[:, :, r, :]),
            "w1s": w1s, "w2s": w2s, "wos": wos,
            "gwh": gwh, "gwt0": gwt0, "gwlast": gwlast, "gbin": gbin,
            "b1h": b1h, "b2g": b2g,
            "ctabin": ctab, "stabin": stab, "maskin": maskh,
            "vmaskin": np.ascontiguousarray(vm),
            "identin": ident, "identfin": identf,
            "is0in": np.array([[1.0 if r == 0 else 0.0]], f),
            "agoffin": np.array([[r * TOK]], np.int32),
        })
    return in_maps


def kernel(**inputs):
    in_maps = _prep_inputs(**inputs)
    nc, _ = build()
    res = run_bass_kernel_spmd(nc, in_maps, core_ids=list(range(NCORES)))
    outs = [np.asarray(res.results[r]["out"], np.float32) for r in range(NCORES)]
    return np.concatenate(outs, axis=0).reshape(1, S, D)


# version 11 (final)
# speedup vs baseline: 1.0286x; 1.0286x over previous
import contextlib
import numpy as np
import ml_dtypes
import concourse.bass as bass
import concourse.mybir as mybir
from concourse.bass_utils import run_bass_kernel_spmd

F32 = mybir.dt.float32
BF16 = mybir.dt.bfloat16
I32 = mybir.dt.int32
AF = mybir.ActivationFunctionType
ALU = mybir.AluOpType
AX = mybir.AxisListType

NCORES = 8
S = 8192
D = 4096
NQH = 4
TOK = 1024
NT = 8
NDC = 32
NTC = 16
NKT = 64
VIS = 2457
ISQ = np.float32(1.0 / np.sqrt(128.0))
NEG_B = np.float32(-20.0)


def f32(v):
    return np.float32(v)


class Sem:
    def __init__(self, es, nc, name):
        self.h = es.enter_context(nc.semaphore(name))
        self.n = 0
        self.name = name


class Prog:
    def __init__(self, nc, es):
        self.nc = nc
        self.es = es
        self._waited = {}

    def sem(self, name):
        return Sem(self.es, self.nc, name)

    def wge(self, eng, sem, val):
        if val <= 0:
            return
        key = (id(eng), id(sem))
        if self._waited.get(key, -1) >= val:
            return
        eng.wait_ge(sem.h, val)
        self._waited[key] = val

    @staticmethod
    def inc(sem, instr, amt=1):
        instr.then_inc(sem.h, amt)
        sem.n += amt
        return instr


def build(dbg=()):
    nc = bass.Bass("TRN2", target_bir_lowering=False, debug=False)
    nc.detect_race_conditions = False
    es = contextlib.ExitStack()
    p = Prog(nc, es)
    dt = nc.dram_tensor

    x = dt("x", [TOK, D], F32, kind="ExternalInput")
    wqh = dt("wqh", [128, NDC, NQH, 128], F32, kind="ExternalInput")
    wkh = dt("wkh", [128, NDC, 128], F32, kind="ExternalInput")
    wvh = dt("wvh", [128, NDC, 128], F32, kind="ExternalInput")
    w1s = dt("w1s", [3, 32, 128, 32, 128], F32, kind="ExternalInput")
    w2s = dt("w2s", [3, 32, 128, 32, 128], F32, kind="ExternalInput")
    wos = dt("wos", [32, 128, 32, 128], F32, kind="ExternalInput")
    gwh = dt("gwh", [128, NDC, 3], F32, kind="ExternalInput")
    gwt0 = dt("gwt0", [1, 3 * D], F32, kind="ExternalInput")
    gwlast = dt("gwlast", [1, 3], F32, kind="ExternalInput")
    gbin = dt("gbin", [1, 3], F32, kind="ExternalInput")
    b1h = dt("b1h", [128, 96], F32, kind="ExternalInput")
    b2g = dt("b2g", [1, 3 * D], F32, kind="ExternalInput")
    ctabin = dt("ctabin", [128, S], F32, kind="ExternalInput")
    stabin = dt("stabin", [128, S], F32, kind="ExternalInput")
    maskin = dt("maskin", [128, 4, 512], BF16, kind="ExternalInput")
    vmaskin = dt("vmaskin", [128, NT], F32, kind="ExternalInput")
    identin = dt("identin", [128, 128], BF16, kind="ExternalInput")
    identfin = dt("identfin", [128, 128], F32, kind="ExternalInput")
    is0in = dt("is0in", [1, 1], F32, kind="ExternalInput")
    agoffin = dt("agoffin", [1, 1], I32, kind="ExternalInput")

    ag1_in = dt("ag1_in", [D, TOK], BF16)
    agxnA = dt("agxnA", [NCORES * (D // 2), TOK], BF16, addr_space="Shared")
    agxnB = dt("agxnB", [NCORES * (D // 2), TOK], BF16, addr_space="Shared")
    ag2_in = dt("ag2_in", [1, 16], F32)
    ag2_out = dt("ag2_out", [NCORES, 16], F32, addr_space="Shared")
    qt_dram = dt("qt_dram", [NQH * 128, S], BF16)
    ago_in = dt("ago_in", [NQH * 128, S], BF16)
    ago_out = dt("ago_out", [NQH * NCORES * 128, S], BF16, addr_space="Shared")
    out = dt("out", [TOK, D], F32, kind="ExternalOutput")

    dbg_t = {}
    for name, shape, dty in [
        ("dbg_xnt", [D, TOK], BF16),
        ("dbg_qt", [NQH * 128, S], BF16),
        ("dbg_kt", [128, S], BF16),
        ("dbg_v", [128, NKT * 128], BF16),
        ("dbg_ago", [NCORES * NQH * 128, S], BF16),
        ("dbg_ag2", [NCORES, 16], F32),
        ("dbg_glog", [128, NT * 3], F32),
        ("dbg_wgt", [128, NT * 3], F32),
    ]:
        if name in dbg:
            dbg_t[name] = dt(name, shape, dty, kind="ExternalOutput")

    def sball(name, shape, dty):
        return es.enter_context(nc.sbuf_tensor(name, shape, dty))

    TE, VE, SE, GE, SY = nc.tensor, nc.vector, nc.scalar, nc.gpsimd, nc.sync


    sem = {n: p.sem(n) for n in [
        "rG", "rS", "x0", "x1", "v1", "a1", "t1", "agw", "cc",
        "agx0", "agx1", "cs0", "cs1", "t2", "v2", "a2", "sqt",
        "qtl0", "qtl1", "qtl2", "qtl3", "t3s", "a3e", "v3m", "t3p", "t3n",
        "a3n", "v3n", "v3a", "t3d",
        "sago0", "sago1",
        "gs0", "gs1", "gs2", "gs3", "bb", "bbg", "t4", "a4", "v4", "sxe", "vacc", "gacc",
        "os0", "os1", "dbgs", "vself", "aself"]}

    def vedge(ins):
        p.inc(sem["vself"], ins, 1)
        p.wge(VE, sem["vself"], sem["vself"].n)
        return ins

    def aedge(ins):
        p.inc(sem["aself"], ins, 1)
        p.wge(SE, sem["aself"], sem["aself"].n)
        return ins

    g_ident = sball("g_ident", [128, 128], BF16)
    g_identf = sball("g_identf", [128, 128], F32)
    g_ones_col = sball("g_ones_col", [128, 1], BF16)
    g_ones_colf = sball("g_ones_colf", [128, 1], F32)
    g_ones_row = sball("g_ones_row", [1, 128], F32)
    g_ones8 = sball("g_ones8", [8, 1], F32)
    g_gw = sball("g_gw", [128, NDC * 3], BF16)
    g_b1 = sball("g_b1", [128, 96], F32)
    g_gwlast = sball("g_gwlast", [1, 3], F32)
    g_gb = sball("g_gb", [1, 3], F32)
    g_vmask = sball("g_vmask", [128, NT], F32)
    g_is0 = sball("g_is0", [1, 1], F32)
    g_glog = sball("g_glog", [128, NT * 3], F32)
    g_ag2 = sball("g_ag2", [1, 16], F32)
    g_glog0 = sball("g_glog0", [1, 3], F32)
    g_mod = sball("g_mod", [1, 2], F32)
    g_agoff = sball("g_agoff", [1, 1], I32)

    for dst, src in [
        (g_ident[:], identin[:]), (g_identf[:], identfin[:]),
        (g_gw[:], gwh.rearrange("p c e -> p (c e)")),
        (g_b1[:], b1h[:]), (g_gwlast[:], gwlast[:]), (g_gb[:], gbin[:]),
        (g_vmask[:], vmaskin[:]), (g_is0[:], is0in[:]),
        (g_agoff[:], agoffin[:]),
    ]:
        p.inc(sem["rG"], GE.dma_start(out=dst, in_=src), 16)
    rg_consts = sem["rG"].n
    ins = None
    for buf, val in [(g_ones_col, 1.0), (g_ones_colf, 1.0), (g_ones_row, 1.0),
                     (g_ones8, 1.0), (g_ag2, 0.0)]:
        ins = VE.memset(buf[:], val)
    p.inc(sem["v1"], ins, 1)

    esA = contextlib.ExitStack()

    def sbA(name, shape, dty):
        return esA.enter_context(nc.sbuf_tensor(name, shape, dty))

    x_sb = sbA("x_sb", [128, 2, D], F32)
    xnf = sbA("xnf", [128, 2, D], F32)
    xnbf = sbA("xnbf", [128, 2, D], BF16)
    junk = sbA("junk", [128, D], BF16)
    xnt = sbA("xnt", [128, NDC, NT, 128], BF16)
    st_s1 = sbA("st_s1", [128, NT], F32)
    st_m = sbA("st_m", [128, NT], F32)
    st_ssq = sbA("st_ssq", [128, NT], F32)
    st_m2 = sbA("st_m2", [128, NT], F32)
    st_var = sbA("st_var", [128, NT], F32)
    st_lnv = sbA("st_lnv", [128, NT], F32)
    st_rstd = sbA("st_rstd", [128, NT], F32)
    st_sabs = sbA("st_sabs", [128, NT], F32)
    vm2 = sbA("vm2", [128, 2], F32)
    g_xn0 = sbA("g_xn0", [1, D], F32)
    esPP = contextlib.ExitStack()
    psA = esPP.enter_context(nc.psum_tensor("psA", [128, 1024], BF16))
    psB = esPP.enter_context(nc.psum_tensor("psB", [128, 1024], BF16))
    gate_ps = esPP.enter_context(nc.psum_tensor("gate_ps", [128, 3], F32))
    mod_ps = esPP.enter_context(nc.psum_tensor("mod_ps", [1, 2], F32))

    xsem = [sem["x0"], sem["x1"]]
    v_xnf = [0] * NT
    v_evac = [[0] * 4 for _ in range(NT)]
    a_sq = [0] * NT
    a_abs = [0] * NT
    t_tr = [[0] * 4 for _ in range(NT)]
    t_gate = [0] * NT
    t_mod = 0
    a_gate = 0

    for tt in range(NT):
        b = tt % 2
        sl = slice(tt, tt + 1)
        if tt >= 2:
            p.wge(SY, sem["a1"], a_abs[tt - 2])
            p.wge(SY, sem["v1"], v_xnf[tt - 2])
        p.inc(xsem[b], SY.dma_start(out=x_sb[:, b, :],
                                    in_=x[tt * 128:(tt + 1) * 128, :]), 16)
        xw = 16 * (tt // 2 + 1)
        p.wge(SE, xsem[b], xw)
        p.inc(sem["a1"], SE.activation(junk[:], x_sb[:, b, :], AF.Square,
                                       accum_out=st_ssq[:, sl]), 1)
        a_sq[tt] = sem["a1"].n
        p.wge(VE, xsem[b], xw)
        vedge(VE.reduce_sum(st_s1[:, sl], x_sb[:, b, :], axis=AX.X))
        vedge(VE.tensor_scalar_mul(st_m[:, sl], st_s1[:, sl], float(1.0 / D)))
        p.wge(VE, sem["a1"], a_sq[tt])
        vedge(VE.tensor_scalar_mul(st_var[:, sl], st_ssq[:, sl], float(1.0 / D)))
        vedge(VE.tensor_scalar(st_m2[:, sl], st_m[:, sl], st_m[:, tt:tt + 1], None, ALU.mult))
        p.inc(sem["v1"], VE.tensor_sub(st_var[:, sl], st_var[:, sl], st_m2[:, sl]), 1)
        v_var = sem["v1"].n
        p.wge(SE, sem["v1"], v_var)
        aedge(SE.activation(st_lnv[:, sl], st_var[:, sl], AF.Ln, bias=f32(1e-5)))
        p.inc(sem["a1"], SE.activation(st_rstd[:, sl], st_lnv[:, sl], AF.Exp,
                                       scale=f32(-0.5)), 1)
        a_rstd = sem["a1"].n
        p.wge(VE, sem["a1"], a_rstd)
        if tt >= 2:
            p.wge(VE, sem["a1"], a_abs[tt - 2])
            p.wge(VE, sem["t1"], t_tr[tt - 2][3])
        VE.tensor_scalar(xnf[:, b, :], x_sb[:, b, :], st_m[:, tt:tt + 1],
                         st_rstd[:, tt:tt + 1], ALU.subtract, ALU.mult)
        ins = VE.tensor_copy(xnbf[:, b, :], xnf[:, b, :])
        if tt == 0:
            ins = VE.tensor_copy(g_xn0[:], xnf[0:1, 0, :])
        p.inc(sem["v1"], ins, 1)
        v_xnf[tt] = sem["v1"].n
        p.wge(SE, sem["v1"], v_xnf[tt])
        p.inc(sem["a1"], SE.activation(junk[:], xnf[:, b, :], AF.Abs,
                                       accum_out=st_sabs[:, sl]), 1)
        a_abs[tt] = sem["a1"].n
        p.wge(VE, sem["a1"], a_abs[tt])
        p.wge(VE, sem["t1"], t_mod)
        VE.tensor_copy(vm2[:, 0:1], st_sabs[:, sl])
        p.inc(sem["v1"], VE.tensor_tensor(vm2[:, 1:2], st_sabs[:, sl],
                                          g_vmask[:, sl], ALU.mult), 1)
        v_vm2 = sem["v1"].n
        p.wge(TE, sem["rG"], rg_consts)
        p.wge(TE, sem["v1"], v_xnf[tt])
        for j in range(4):
            pbuf = psA if j % 2 == 0 else psB
            gidx = tt * 4 + j
            if gidx >= 2:
                ptt, pj = (gidx - 2) // 4, (gidx - 2) % 4
                p.wge(TE, sem["v1"], v_evac[ptt][pj])
            last = None
            for c in range(8):
                dc = 8 * j + c
                last = TE.transpose(pbuf[:, c * 128:(c + 1) * 128],
                                    xnbf[:, b, dc * 128:(dc + 1) * 128], g_ident[:])
            p.inc(sem["t1"], last, 1)
            t_tr[tt][j] = sem["t1"].n
            p.wge(VE, sem["t1"], t_tr[tt][j])
            p.inc(sem["v1"], VE.tensor_copy(
                xnt[:, 8 * j:8 * j + 8, tt, :],
                pbuf.rearrange("p (c t) -> p c t", c=8)), 1)
            v_evac[tt][j] = sem["v1"].n
        p.wge(TE, sem["v1"], v_evac[tt][3])
        if tt >= 1:
            p.wge(TE, sem["a2"], a_gate)
        lastg = None
        for dc in range(NDC):
            lastg = TE.matmul(gate_ps[:], xnt[:, dc, tt, :],
                              g_gw[:, dc * 3:(dc + 1) * 3],
                              start=(dc == 0), stop=(dc == NDC - 1))
        p.inc(sem["t1"], lastg, 1)
        t_gate[tt] = sem["t1"].n
        p.wge(SE, sem["t1"], t_gate[tt])
        p.inc(sem["a2"], SE.copy(g_glog[:, tt * 3:(tt + 1) * 3], gate_ps[:]), 1)
        a_gate = sem["a2"].n
        p.wge(TE, sem["v1"], v_vm2)
        p.inc(sem["t1"], TE.matmul(mod_ps[:], g_ones_colf[:], vm2[:],
                                   start=(tt == 0), stop=(tt == NT - 1)), 1)
        t_mod = sem["t1"].n
    p.wge(SE, sem["t1"], t_mod)
    p.inc(sem["a1"], SE.copy(g_mod[:], mod_ps[:]), 1)
    a_mod = sem["a1"].n
    a1_p1 = sem["a1"].n
    t1_p1 = sem["t1"].n

    p.wge(SY, sem["v1"], v_evac[NT - 1][3])
    for dcg in range(4):
        p.inc(sem["agw"], SY.dma_start(
            out=ag1_in.rearrange("(c p) t -> p c t", p=128)[:, 8 * dcg:8 * dcg + 8, :],
            in_=xnt[:, 8 * dcg:8 * dcg + 8, :, :].rearrange("p c u t -> p c (u t)")), 16)
    p.wge(GE, sem["agw"], 32)
    p.inc(sem["cc"], GE.collective_compute(
        "AllGather", ALU.bypass, replica_groups=[list(range(NCORES))],
        ins=[ag1_in[0:D // 2, :]], outs=[agxnA[:]]), 1)
    p.wge(GE, sem["agw"], sem["agw"].n)
    p.inc(sem["cc"], GE.collective_compute(
        "AllGather", ALU.bypass, replica_groups=[list(range(NCORES))],
        ins=[ag1_in[D // 2:D, :]], outs=[agxnB[:]]), 1)

    gwt0_sb = sbA("gwt0_sb", [1, D], F32)
    dot_t = sbA("dot_t", [1, D], F32)
    for e in range(3):
        if e > 0:
            p.wge(SY, sem["v1"], sem["v1"].n)
        p.inc(sem["rS"], SY.dma_start(out=gwt0_sb[:], in_=gwt0[0:1, e * D:(e + 1) * D]), 16)
        p.wge(VE, sem["rS"], sem["rS"].n)
        p.wge(VE, sem["v1"], v_xnf[0])
        VE.tensor_tensor(dot_t[:], g_xn0[:], gwt0_sb[:], ALU.mult)
        ins = VE.reduce_sum(g_glog0[0:1, e:e + 1], dot_t[:], axis=AX.X)
        p.inc(sem["v1"], ins, 1)
        p.wge(VE, sem["v1"], sem["v1"].n)
    p.wge(VE, sem["a1"], a_mod)
    VE.tensor_scalar(g_ag2[0:1, 0:3], g_glog0[:], g_is0[0:1, 0:1], None, ALU.mult)
    p.inc(sem["v1"], VE.tensor_copy(g_ag2[0:1, 3:5], g_mod[:]), 1)
    v_ag2 = sem["v1"].n
    p.wge(GE, sem["v1"], v_ag2)
    p.inc(sem["rG"], GE.dma_start(out=ag2_in[:], in_=g_ag2[:]), 16)
    p.wge(GE, sem["rG"], sem["rG"].n)
    p.inc(sem["cc"], GE.collective_compute(
        "AllGather", ALU.bypass, replica_groups=[list(range(NCORES))],
        ins=[ag2_in[:]], outs=[ag2_out[:]]), 1)

    if "dbg_xnt" in dbg_t:
        p.wge(GE, sem["agw"], sem["agw"].n)
        p.inc(sem["dbgs"], GE.dma_start(out=dbg_t["dbg_xnt"][:], in_=ag1_in[:]), 16)
    if "dbg_ag2" in dbg_t:
        p.wge(GE, sem["cc"], 3)
        p.inc(sem["dbgs"], GE.dma_start(out=dbg_t["dbg_ag2"][:], in_=ag2_out[:]), 16)
    if "dbg_glog" in dbg_t:
        gl_tmp = dt("gl_tmp", [128, NT * 3], F32)
        p.wge(GE, sem["a2"], a_gate)
        p.inc(sem["dbgs"], GE.dma_start(out=gl_tmp[:], in_=g_glog[:]), 16)
        p.wge(GE, sem["dbgs"], sem["dbgs"].n)
        p.inc(sem["dbgs"], GE.dma_start(out=dbg_t["dbg_glog"][:], in_=gl_tmp[:]), 16)

    esA.close()
    esPP.close()

    esKV = contextlib.ExitStack()
    kt_sb = esKV.enter_context(nc.sbuf_tensor("kt_sb", [128, S], BF16))
    v_sb = esKV.enter_context(nc.sbuf_tensor("v_sb", [128, NKT, 128], BF16))
    esB = contextlib.ExitStack()

    def sbB(name, shape, dty):
        return esB.enter_context(nc.sbuf_tensor(name, shape, dty))

    wq_sb = sbB("wq_sb", [128, NDC, NQH, 128], BF16)
    wk_sb = sbB("wk_sb", [128, NDC, 128], BF16)
    wv_sb = sbB("wv_sb", [128, NDC, 128], BF16)
    agx = sbB("agx", [128, 3, 16, 1024], BF16)
    csb = sbB("csb", [128, 2, 2, 512], F32)
    stg_n = sbB("stg_n", [128, 2, 512], F32)
    stg_s = sbB("stg_s", [128, 2, 512], F32)
    rt1 = sbB("rt1", [128, 512], F32)
    rt2 = sbB("rt2", [128, 512], F32)
    qtile = sbB("qtile", [128, 2, 512], BF16)
    vt_sb = sbB("vt_sb", [128, 512], BF16)
    esP2 = contextlib.ExitStack()
    qps = [esP2.enter_context(nc.psum_tensor("qps%d" % i, [128, 512], F32))
           for i in range(NQH)]
    kps = esP2.enter_context(nc.psum_tensor("kps", [128, 512], F32))
    vtps = esP2.enter_context(nc.psum_tensor("vtps", [128, 512], F32))
    vtrps = esP2.enter_context(nc.psum_tensor("vtrps", [128, 512], BF16))

    p.wge(GE, sem["agw"], 64)
    p.wge(GE, sem["v1"], v_ag2)
    p.wge(GE, sem["a1"], a1_p1)
    p.wge(GE, sem["a2"], a_gate)
    p.wge(GE, sem["t1"], t1_p1)
    for dst, src in [
        (wq_sb.rearrange("p a b c -> p (a b c)"), wqh.rearrange("p a b c -> p (a b c)")),
        (wk_sb.rearrange("p a b -> p (a b)"), wkh.rearrange("p a b -> p (a b)")),
        (wv_sb.rearrange("p a b -> p (a b)"), wvh.rearrange("p a b -> p (a b)")),
    ]:
        p.inc(sem["rG"], GE.dma_start(out=dst, in_=src), 16)
    rg_w2 = sem["rG"].n

    css = [sem["cs0"], sem["cs1"]]
    t_half = {}
    t_vtr = {}
    a_vt = {}
    a_ropecp = {}
    v_rope = {}
    v_ropeseq = {}
    v_vsb = {}
    s_qtq = {}
    rope_seq = 0
    qseq = 0

    agxn_vs = [t.rearrange("(r c p) t -> r p c t", r=NCORES, p=128)
               for t in (agxnA, agxnB)]

    def emit_agx_load(u):
        if u >= NTC:
            return
        ptc, h = u // 2, u % 2
        bu = u % 3
        if u >= 3:
            i2c = (2 * ((u - 3) // 2) + 1) * 2 + (u - 3) % 2
            p.wge(SY, sem["t2"], t_half[i2c])
            p.wge(GE, sem["t2"], t_half[i2c])
        p.wge(SY, sem["cc"], 1 + h)
        p.wge(GE, sem["cc"], 1 + h)
        p.inc(sem["agx0"], SY.dma_start(
            out=agx[:, bu, 0:8, :], in_=agxn_vs[h][ptc, :, 0:8, :]), 16)
        p.inc(sem["agx1"], GE.dma_start(
            out=agx[:, bu, 8:16, :], in_=agxn_vs[h][ptc, :, 8:16, :]), 16)

    for u in range(3):
        emit_agx_load(u)

    for tc in range(NTC):
        bcs = tc % 2
        if tc == 0:
            p.wge(SE, sem["t1"], t1_p1)
            p.wge(SE, sem["v1"], v_ag2)
            p.wge(SE, sem["agw"], 64)
        if tc >= 2:
            p.wge(SE, sem["v2"], v_rope[(tc - 2, 4)])
        p.inc(css[bcs], SE.dma_start(out=csb[:, bcs, 0, :],
                                     in_=ctabin[:, tc * 512:(tc + 1) * 512]), 16)
        p.inc(css[bcs], SE.dma_start(out=csb[:, bcs, 1, :],
                                     in_=stabin[:, tc * 512:(tc + 1) * 512]), 16)
        p.wge(TE, sem["rG"], rg_w2)
        if tc == 0:
            p.wge(TE, sem["a2"], a_gate)
            p.wge(TE, sem["a1"], a1_p1)
        for h in range(2):
            i2 = tc * 2 + h
            u = (tc // 2) * 2 + h
            p.wge(TE, sem["agx0"], 16 * (u + 1))
            p.wge(TE, sem["agx1"], 16 * (u + 1))
            if tc > 0 and h == 0:
                p.wge(TE, sem["a2"], a_ropecp[(tc - 1, 4)])
                p.wge(TE, sem["a2"], a_vt[tc - 1])
            last = None
            c0 = (tc % 2) * 512
            for dchalf in range(16):
                dc = 16 * h + dchalf
                rhs = agx[:, u % 3, dchalf, c0:c0 + 512]
                for qh in range(NQH):
                    TE.matmul(qps[qh][:], wq_sb[:, dc, qh, :], rhs,
                              start=(dc == 0), stop=(dc == NDC - 1))
                TE.matmul(kps[:], wk_sb[:, dc, :], rhs,
                          start=(dc == 0), stop=(dc == NDC - 1))
                last = TE.matmul(vtps[:], wv_sb[:, dc, :], rhs,
                                 start=(dc == 0), stop=(dc == NDC - 1))
            p.inc(sem["t2"], last, 1)
            t_half[i2] = sem["t2"].n
            if tc % 2 == 1:
                emit_agx_load(u + 3)
        p.wge(SE, sem["t2"], t_half[tc * 2 + 1])
        if tc >= 1:
            p.wge(SE, sem["t2"], t_vtr[tc - 1])
        p.inc(sem["a2"], SE.copy(vt_sb[:], vtps[:]), 1)
        a_vt[tc] = sem["a2"].n
        p.wge(TE, sem["a2"], a_vt[tc])
        if tc >= 1:
            p.wge(TE, sem["v2"], v_vsb[tc - 1])
        lastv = None
        for j in range(4):
            lastv = TE.transpose(vtrps[:, j * 128:(j + 1) * 128],
                                 vt_sb[:, j * 128:(j + 1) * 128], g_ident[:])
        p.inc(sem["t2"], lastv, 1)
        t_vtr[tc] = sem["t2"].n
        p.wge(VE, sem["t2"], t_vtr[tc])
        p.inc(sem["v2"], VE.tensor_copy(
            v_sb[:, tc * 4:tc * 4 + 4, :].rearrange("p a b -> p (a b)"), vtrps[:]), 1)
        v_vsb[tc] = sem["v2"].n
        for head in range(5):
            src_ps = qps[head] if head < 4 else kps
            rb = rope_seq % 2
            p.wge(SE, sem["t2"], t_half[tc * 2 + 1])
            if rope_seq >= 2:
                p.wge(SE, sem["v2"], v_ropeseq[rope_seq - 2])
            SE.copy(stg_n[:, rb, :], src_ps[:])
            SE.copy(stg_s[0:64, rb, :], src_ps[64:128, :])
            p.inc(sem["a2"], SE.copy(stg_s[64:128, rb, :], src_ps[0:64, :]), 1)
            a_ropecp[(tc, head)] = sem["a2"].n
            p.wge(VE, sem["a2"], a_ropecp[(tc, head)])
            p.wge(VE, css[bcs], 32 * (tc // 2 + 1))
            if head < 4:
                qb = qseq % 2
                dst = qtile[:, qb, :]
                if qseq >= 2:
                    p.wge(VE, sem["sqt"], s_qtq[qseq - 2])
            else:
                dst = kt_sb[:, tc * 512:(tc + 1) * 512]
            VE.tensor_tensor(rt1[:], stg_n[:, rb, :], csb[:, bcs, 0, :], ALU.mult)
            VE.tensor_tensor(rt2[:], stg_s[:, rb, :], csb[:, bcs, 1, :], ALU.mult)
            ins = VE.tensor_tensor(dst, rt1[:], rt2[:], ALU.add)
            p.inc(sem["v2"], ins, 1)
            v_rope[(tc, head)] = sem["v2"].n
            v_ropeseq[rope_seq] = sem["v2"].n
            if head < 4:
                p.wge(SY, sem["v2"], v_rope[(tc, head)])
                p.inc(sem["sqt"], SY.dma_start(
                    out=qt_dram[head * 128:(head + 1) * 128, tc * 512:(tc + 1) * 512],
                    in_=qtile[:, qseq % 2, :]), 16)
                s_qtq[qseq] = sem["sqt"].n
                qseq += 1
            rope_seq += 1
    v2_p2 = sem["v2"].n
    a2_p2 = sem["a2"].n
    sqt_total = sem["sqt"].n
    esB.close()
    esP2.close()

    if "dbg_qt" in dbg_t:
        p.wge(GE, sem["sqt"], sqt_total)
        p.inc(sem["dbgs"], GE.dma_start(out=dbg_t["dbg_qt"][:], in_=qt_dram[:]), 16)
    if "dbg_kt" in dbg_t:
        kt_tmp = dt("kt_tmp", [128, S], BF16)
        p.wge(GE, sem["v2"], v2_p2)
        p.inc(sem["dbgs"], GE.dma_start(out=kt_tmp[:], in_=kt_sb[:]), 16)
        p.wge(GE, sem["dbgs"], sem["dbgs"].n)
        p.inc(sem["dbgs"], GE.dma_start(out=dbg_t["dbg_kt"][:], in_=kt_tmp[:]), 16)
    if "dbg_v" in dbg_t:
        v_tmp = dt("v_tmp", [128, NKT * 128], BF16)
        p.wge(GE, sem["v2"], v2_p2)
        p.inc(sem["dbgs"], GE.dma_start(out=v_tmp[:],
                                        in_=v_sb.rearrange("p a b -> p (a b)")), 16)
        p.wge(GE, sem["dbgs"], sem["dbgs"].n)
        p.inc(sem["dbgs"], GE.dma_start(out=dbg_t["dbg_v"][:], in_=v_tmp[:]), 16)

    esC = contextlib.ExitStack()

    def sbC(name, shape, dty):
        return esC.enter_context(nc.sbuf_tensor(name, shape, dty))

    g_mask = sbC("g_mask", [128, 4, 512], BF16)
    qtb = sbC("qtb", [128, 4, 512], BF16)
    pt_sb = sbC("pt_sb", [128, 4, 1024], BF16)
    obuf = sbC("obuf", [128, 2, 512], BF16)
    lntmp = sbC("lntmp", [1, 512], F32)
    rc_sb = sbC("rc_sb", [1, 2, 512], F32)
    rcb_sb = sbC("rcb_sb", [128, 2, 512], F32)
    tmpb3 = sbC("tmpb3", [128, 2, 512], BF16)
    tmpq3 = sbC("tmpq3", [128, 2, NTC, 512], BF16)
    esP3 = contextlib.ExitStack()
    stps = [esP3.enter_context(nc.psum_tensor("stps%d" % i, [128, 1024], F32))
            for i in range(2)]
    ops = [esP3.enter_context(nc.psum_tensor("ops%d" % i, [128, 512], F32))
           for i in range(2)]
    rcbps = [esP3.enter_context(nc.psum_tensor("rcbps%d" % i, [128, 512], F32))
             for i in range(2)]

    p.wge(GE, sem["t2"], sem["t2"].n)
    p.wge(GE, sem["v2"], v2_p2)
    p.wge(GE, sem["a2"], a2_p2)
    p.wge(GE, sem["sqt"], sqt_total)
    p.inc(sem["rG"], GE.dma_start(out=g_mask.rearrange("p k q -> p (k q)"),
                                  in_=maskin.rearrange("p k q -> p (k q)")), 16)
    rg_mask = sem["rG"].n

    qtls = [sem["qtl0"], sem["qtl1"], sem["qtl2"], sem["qtl3"]]
    sagos = [sem["sago0"], sem["sago1"]]
    groups = [(tc, qh) for qh in range(NQH) for tc in range(NTC)]
    bi = 0
    t3s_at = {}
    a3e_at = {}
    t3p_at = {}
    grp_last_b = {}
    v3n_at = {}
    a3n_at = {}
    v3a_at = {}
    t3d_at = {}
    t3n_rcb = {}
    pend = []

    def emit_pv(it):
        (bb, j, g0, lb, first, last_it, diag, mi) = it
        if diag:
            p.wge(TE, sem["v3m"], mi)
        else:
            p.wge(TE, sem["a3e"], a3e_at[bb])
        if first:
            p.wge(TE, sem["v3n"], v3n_at.get(j - 2, 0))
        TE.matmul(ops[j % 2][:], v_sb[:, g0, :], pt_sb[:, bb % 4, 0:512],
                  start=first, stop=False)
        ins = TE.matmul(ops[j % 2][:], v_sb[:, g0 + 1, :], pt_sb[:, bb % 4, 512:1024],
                        start=False, stop=last_it)
        p.inc(sem["t3p"], ins, 1)
        t3p_at[bb] = sem["t3p"].n

    for j, (tc, qh) in enumerate(groups):
        if j >= 4:
            p.wge(SY, sem["t3s"], t3s_at[grp_last_b[j - 4]])
        if j == 0:
            p.wge(SY, sem["sqt"], sqt_total)
        p.inc(qtls[j % 4], SY.dma_start(
            out=qtb[:, j % 4, :],
            in_=qt_dram[qh * 128:(qh + 1) * 128, tc * 512:(tc + 1) * 512]), 16)
        nb = 2 * (tc + 1)
        for lb in range(nb):
            g0 = 2 * lb
            first, last_it = lb == 0, lb == nb - 1
            diag = g0 // 4 == tc
            if bi == 0:
                p.wge(TE, sem["v2"], v2_p2)
                p.wge(TE, sem["a2"], a2_p2)
                p.wge(TE, sem["rG"], rg_mask)
            p.wge(TE, qtls[j % 4], 16 * (j // 4 + 1))
            p.wge(TE, sem["a3e"], a3e_at.get(bi - 2, 0))
            TE.matmul(stps[bi % 2][:, 0:512], kt_sb[:, g0 * 128:(g0 + 1) * 128],
                      qtb[:, j % 4, :], start=True, stop=True)
            ins = TE.matmul(stps[bi % 2][:, 512:1024],
                            kt_sb[:, (g0 + 1) * 128:(g0 + 2) * 128],
                            qtb[:, j % 4, :], start=True, stop=True)
            p.inc(sem["t3s"], ins, 1)
            t3s_at[bi] = sem["t3s"].n
            p.wge(SE, sem["t3s"], t3s_at[bi])
            p.wge(SE, sem["t3p"], t3p_at.get(bi - 4, 0))
            p.wge(SE, sem["v3a"], v3a_at.get(bi - 4, 0))
            ins = SE.activation(pt_sb[:, bi % 4, :], stps[bi % 2][:], AF.Exp,
                                scale=ISQ, bias=NEG_B)
            p.inc(sem["a3e"], ins, 1)
            a3e_at[bi] = sem["a3e"].n
            mi = 0
            p.wge(VE, sem["a3e"], a3e_at[bi])
            if diag:
                VE.tensor_tensor(pt_sb[:, bi % 4, 0:512], pt_sb[:, bi % 4, 0:512],
                                 g_mask[:, (g0 % 4), :], ALU.mult)
                ins = VE.tensor_tensor(pt_sb[:, bi % 4, 512:1024],
                                       pt_sb[:, bi % 4, 512:1024],
                                       g_mask[:, (g0 % 4) + 1, :], ALU.mult)
                p.inc(sem["v3m"], ins, 1)
                mi = sem["v3m"].n
            ins = VE.tensor_tensor(tmpb3[:, bi % 2, :], pt_sb[:, bi % 4, 0:512],
                                   pt_sb[:, bi % 4, 512:1024], ALU.add)
            p.inc(sem["v3a"], ins, 1)
            if lb % 2 == 1:
                if lb == 1:
                    p.wge(VE, sem["t3d"], t3d_at.get(j - 2, 0))
                ins = VE.tensor_tensor(tmpq3[:, j % 2, lb // 2, :], tmpb3[:, 0, :],
                                       tmpb3[:, 1, :], ALU.add)
                p.inc(sem["v3a"], ins, 1)
            v3a_at[bi] = sem["v3a"].n
            if pend:
                emit_pv(pend.pop())
            pend.append((bi, j, g0, lb, first, last_it, diag, mi))
            bi += 1
        grp_last_b[j] = bi - 1
        emit_pv(pend.pop())
        p.wge(TE, sem["v3a"], v3a_at[grp_last_b[j]])
        p.wge(TE, sem["a3n"], a3n_at.get(j - 2, 0))
        nq = nb // 2
        ins = None
        for qi in range(nq):
            ins = TE.matmul(rcbps[j % 2][0:1, :], g_ones_col[:],
                            tmpq3[:, j % 2, qi, :],
                            start=(qi == 0), stop=(qi == nq - 1))
        p.inc(sem["t3d"], ins, 1)
        t3d_at[j] = sem["t3d"].n
        p.wge(SE, sem["t3d"], t3d_at[j])
        SE.activation(lntmp[:], rcbps[j % 2][0:1, :], AF.Ln)
        ins = SE.activation(rc_sb[0:1, j % 2, :], lntmp[0:1, :], AF.Exp,
                            scale=f32(-1.0))
        p.inc(sem["a3n"], ins, 1)
        a3n_exp = sem["a3n"].n
        p.wge(TE, sem["a3n"], a3n_exp)
        ins = TE.matmul(rcbps[j % 2][:], g_ones_row[:], rc_sb[0:1, j % 2, :],
                        start=True, stop=True)
        p.inc(sem["t3n"], ins, 1)
        t3n_rcb[j] = sem["t3n"].n
        p.wge(SE, sem["t3n"], t3n_rcb[j])
        p.inc(sem["a3n"], SE.copy(rcb_sb[:, j % 2, :], rcbps[j % 2][:]), 1)
        a3n_at[j] = sem["a3n"].n
        p.wge(VE, sem["a3n"], a3n_at[j])
        p.wge(VE, sagos[j % 2], 16 * (j // 2))
        p.wge(VE, sem["t3p"], t3p_at[grp_last_b[j]])
        ins = VE.tensor_tensor(obuf[:, j % 2, :], ops[j % 2][:],
                               rcb_sb[:, j % 2, :], ALU.mult)
        p.inc(sem["v3n"], ins, 1)
        v3n_at[j] = sem["v3n"].n
        p.wge(SY, sem["v3n"], v3n_at[j])
        p.inc(sagos[j % 2], SY.dma_start(
            out=ago_in[qh * 128:(qh + 1) * 128, tc * 512:(tc + 1) * 512],
            in_=obuf[:, j % 2, :]), 16)
        if tc == NTC - 1:
            p.wge(GE, sagos[0], sagos[0].n)
            p.wge(GE, sagos[1], sagos[1].n)
            p.inc(sem["cc"], GE.collective_compute(
                "AllGather", ALU.bypass, replica_groups=[list(range(NCORES))],
                ins=[ago_in[qh * 128:(qh + 1) * 128, :]],
                outs=[ago_out[qh * NCORES * 128:(qh + 1) * NCORES * 128, :]]), 1)
    if "dbg_ago" in dbg_t:
        p.wge(GE, sem["cc"], 7)
        for qh in range(NQH):
            p.wge(GE, sem["dbgs"], sem["dbgs"].n)
            p.inc(sem["dbgs"], GE.dma_start(
                out=dbg_t["dbg_ago"].rearrange("(r h p) t -> h (r p) t", h=NQH, p=128)[qh],
                in_=ago_out.rearrange("(h rp) t -> h rp t", h=NQH)[qh]), 16)
    esC.close()
    esP3.close()
    esKV.close()

    esD = contextlib.ExitStack()

    def sbD(name, shape, dty):
        return esD.enter_context(nc.sbuf_tensor(name, shape, dty))

    esPG = contextlib.ExitStack()
    esPGa = contextlib.ExitStack()
    agg_ps = esPGa.enter_context(nc.psum_tensor("agg_ps", [1, 16], F32))
    g9_ps = esPGa.enter_context(nc.psum_tensor("g9_ps", [128, 9], F32))
    wt_ps = esPGa.enter_context(nc.psum_tensor("wt_ps", [1, 384], F32))

    ag2o_sb = sbD("ag2o_sb", [8, 16], F32)
    agg_sb = sbD("agg_sb", [1, 16], F32)
    tiny = sbD("tiny", [1, 8], F32)
    rvec = sbD("rvec", [1, 3], F32)
    glog0f = sbD("glog0f", [1, 3], F32)
    imax = sbD("imax", [1, 3], F32)
    imin_ = sbD("imin_", [1, 3], F32)
    imed = sbD("imed", [1, 3], F32)
    rb_sb = sbD("rb_sb", [128, 9], F32)
    t3a = sbD("t3a", [128, 3], F32)
    e3b = sbD("e3b", [128, 3], F32)
    s3 = sbD("s3", [128, 1], F32)
    nmx = sbD("nmx", [128, 1], F32)
    w0v = sbD("w0v", [128, 1], F32)
    w1v = sbD("w1v", [128, 1], F32)
    sc1 = sbD("sc1", [128, 1], F32)
    sc2 = sbD("sc2", [128, 1], F32)
    wgt_sb = sbD("wgt_sb", [128, NT * 3], F32)
    wgtT = [sbD("wgtT%d" % e, [1, TOK], F32) for e in range(3)]
    wgtTb = [sbD("wgtTb%d" % e, [1, TOK], BF16) for e in range(3)]
    wgtb = [sbD("wgtb%d" % e, [128, 512], F32) for e in range(3)]

    p.wge(GE, sem["cc"], 3)
    p.wge(GE, sem["v3n"], 64)
    p.wge(GE, sagos[0], sagos[0].n)
    p.wge(GE, sagos[1], sagos[1].n)
    p.inc(sem["rG"], GE.dma_start(out=ag2o_sb[:], in_=ag2_out[:]), 16)
    p.wge(TE, sem["rG"], sem["rG"].n)
    p.inc(sem["t4"], TE.matmul(agg_ps[:], g_ones8[:], ag2o_sb[:],
                               start=True, stop=True), 1)
    p.wge(SE, sem["t4"], sem["t4"].n)
    p.inc(sem["a4"], SE.copy(agg_sb[:], agg_ps[:]), 1)
    p.wge(VE, sem["a4"], sem["a4"].n)
    vedge(VE.tensor_tensor(tiny[0:1, 0:1], agg_sb[0:1, 3:4], agg_sb[0:1, 4:5], ALU.subtract))
    vedge(VE.tensor_scalar_mul(tiny[0:1, 1:2], agg_sb[0:1, 4:5], float(1.0 / (VIS * D))))
    vedge(VE.tensor_scalar_mul(tiny[0:1, 2:3], tiny[0:1, 0:1], float(1.0 / ((S - VIS) * D))))
    vedge(VE.tensor_scalar_add(tiny[0:1, 3:4], tiny[0:1, 1:2], float(1e-8)))
    vedge(VE.reciprocal(tiny[0:1, 4:5], tiny[0:1, 3:4]))
    vedge(VE.tensor_tensor(tiny[0:1, 5:6], tiny[0:1, 2:3], tiny[0:1, 4:5], ALU.mult))
    vedge(VE.tensor_scalar(rvec[:], g_gwlast[:], tiny[0:1, 5:6], None, ALU.mult))
    vedge(VE.tensor_tensor(rvec[:], rvec[:], g_gb[:], ALU.add))
    vedge(VE.tensor_tensor(glog0f[:], agg_sb[0:1, 0:3], rvec[:], ALU.add))
    vedge(VE.reduce_max(tiny[0:1, 6:7], glog0f[:], axis=AX.X))
    vedge(VE.tensor_reduce(tiny[0:1, 7:8], glog0f[:], axis=AX.X, op=ALU.min))
    vedge(VE.tensor_scalar(imax[:], glog0f[:], tiny[0:1, 6:7], None, ALU.is_equal))
    vedge(VE.tensor_scalar(imin_[:], glog0f[:], tiny[0:1, 7:8], None, ALU.is_equal))
    vedge(VE.tensor_tensor(imed[:], imax[:], imin_[:], ALU.add))
    ins = VE.tensor_scalar(imed[:], imed[:], -1.0, 1.0, ALU.mult, ALU.add)
    p.inc(sem["v4"], ins, 1)
    p.wge(VE, sem["v4"], sem["v4"].n)
    p.wge(TE, sem["v4"], sem["v4"].n)
    TE.matmul(g9_ps[:, 0:3], g_ones_row[:], rvec[:], start=True, stop=True)
    TE.matmul(g9_ps[:, 3:6], g_ones_row[:], imax[:], start=True, stop=True)
    p.inc(sem["t4"], TE.matmul(g9_ps[:, 6:9], g_ones_row[:], imed[:],
                               start=True, stop=True), 1)
    p.wge(SE, sem["t4"], sem["t4"].n)
    p.inc(sem["a4"], SE.copy(rb_sb[:], g9_ps[:]), 1)
    a4_rb = sem["a4"].n
    a4_wT = 0
    for tt in range(NT):
        sl3 = slice(tt * 3, tt * 3 + 3)
        p.wge(VE, sem["a4"], a4_rb)
        vedge(VE.tensor_tensor(t3a[:], g_glog[:, sl3], rb_sb[:, 0:3], ALU.add))
        vedge(VE.reduce_max(nmx[:], t3a[:], axis=AX.X))
        ins = VE.tensor_scalar_mul(nmx[:], nmx[:], -1.0)
        p.inc(sem["v4"], ins, 1)
        p.wge(VE, sem["v4"], sem["v4"].n)
        p.wge(SE, sem["v4"], sem["v4"].n)
        p.inc(sem["a4"], SE.activation(e3b[:], t3a[:], AF.Exp, bias=nmx[:, 0:1],
                                       accum_out=s3[:]), 1)
        p.wge(VE, sem["a4"], sem["a4"].n)
        vedge(VE.reciprocal(sc1[:], s3[:]))
        vedge(VE.reduce_max(sc2[:], e3b[:], axis=AX.X))
        vedge(VE.tensor_tensor(w0v[:], sc2[:], sc1[:], ALU.mult))
        vedge(VE.tensor_reduce(w1v[:], e3b[:], axis=AX.X, op=ALU.min))
        vedge(VE.tensor_tensor(s3[:], s3[:], sc2[:], ALU.subtract))
        vedge(VE.tensor_tensor(s3[:], s3[:], w1v[:], ALU.subtract))
        vedge(VE.tensor_tensor(w1v[:], s3[:], sc1[:], ALU.mult))
        vedge(VE.tensor_tensor(sc2[:], w0v[:], w1v[:], ALU.add))
        vedge(VE.tensor_scalar_add(sc2[:], sc2[:], float(1e-8)))
        vedge(VE.reciprocal(sc1[:], sc2[:]))
        vedge(VE.tensor_tensor(w0v[:], w0v[:], sc1[:], ALU.mult))
        vedge(VE.tensor_tensor(w1v[:], w1v[:], sc1[:], ALU.mult))
        vedge(VE.tensor_scalar(t3a[:], rb_sb[:, 3:6], w0v[:, 0:1], None, ALU.mult))
        vedge(VE.tensor_scalar(e3b[:], rb_sb[:, 6:9], w1v[:, 0:1], None, ALU.mult))
        ins = VE.tensor_tensor(wgt_sb[:, sl3], t3a[:], e3b[:], ALU.add)
        p.inc(sem["v4"], ins, 1)
        p.wge(VE, sem["v4"], sem["v4"].n)
        v4_wgt = sem["v4"].n
        p.wge(TE, sem["v4"], v4_wgt)
        p.wge(TE, sem["a4"], a4_wT)
        lastt = None
        for e in range(3):
            lastt = TE.transpose(wt_ps[0:1, e * 128:(e + 1) * 128],
                                 wgt_sb[:, tt * 3 + e:tt * 3 + e + 1], g_identf[:])
        p.inc(sem["t4"], lastt, 1)
        p.wge(SE, sem["t4"], sem["t4"].n)
        ins = None
        for e in range(3):
            ins = SE.copy(wgtT[e][0:1, tt * 128:(tt + 1) * 128],
                          wt_ps[0:1, e * 128:(e + 1) * 128])
        p.inc(sem["a4"], ins, 1)
        a4_wT = sem["a4"].n
    p.wge(VE, sem["a4"], a4_wT)
    ins = None
    for e in range(3):
        ins = VE.tensor_copy(wgtTb[e][:], wgtT[e][:])
    p.inc(sem["v4"], ins, 1)
    v4_wTb = sem["v4"].n
    if "dbg_wgt" in dbg_t:
        wg_tmp = dt("wg_tmp", [128, NT * 3], F32)
        p.wge(GE, sem["v4"], v4_wTb)
        p.inc(sem["dbgs"], GE.dma_start(out=wg_tmp[:], in_=wgt_sb[:]), 16)
        p.wge(GE, sem["dbgs"], sem["dbgs"].n)
        p.inc(sem["dbgs"], GE.dma_start(out=dbg_t["dbg_wgt"][:], in_=wg_tmp[:]), 16)

    acts = sbD("acts", [128, 3, 32, 512], BF16)
    bigbuf = sbD("bigbuf", [128, 32, 512], BF16)
    slab = sbD("slab", [128, 4, 32, 128], BF16)
    b2sl = sbD("b2sl", [1, 2, 3, 128], BF16)
    xsl = sbD("xsl", [128, 4, 512], F32)
    ostg = sbD("ostg", [128, 2, 512], F32)
    stge = sbD("stge", [128, 2, 512], F32)
    esPGa.close()
    mm_ps = [esPG.enter_context(nc.psum_tensor("mm_ps%d" % i, [128, 512], F32))
             for i in range(2)]
    pst = [esPG.enter_context(nc.psum_tensor("pst%d" % i, [128, 512], F32))
           for i in range(4)]
    wb_ps = pst[0]

    gss = [sem["gs0"], sem["gs1"], sem["gs2"], sem["gs3"]]
    slab_idx = 0
    slab_cons = {}
    t4_w1 = {}
    a4_act = {}
    a4_ev = {}
    t4_tp = {}
    b2_cons = {}
    v4_dcg = {}
    xsl_load = {}
    os_seq = 0
    oss = [sem["os0"], sem["os1"]]
    a4_wb = 0
    t4_dout_last = 0

    ago_rbase = GE.alloc_register("ago_base")
    ago_rtmp = GE.alloc_register("ago_tmp")
    p.wge(GE, sem["rG"], rg_consts)
    GE.reg_load(ago_rbase, g_agoff[0:1, 0:1])

    for qc in range(2):
        if qc == 0:
            p.wge(SY, sem["v3n"], 64)
            p.wge(SY, sagos[0], sagos[0].n)
            p.wge(SY, sagos[1], sagos[1].n)
        if qc > 0:
            p.wge(SY, sem["t4"], t4_dout_last)
        for dcg8 in range(8):
            p.inc(sem["bb"], SY.dma_start(
                out=bigbuf[:, 4 * dcg8:4 * dcg8 + 4, :],
                in_=ag1_in.rearrange("(c p) t -> p c t", p=128)[
                    :, 4 * dcg8:4 * dcg8 + 4, qc * 512:(qc + 1) * 512]), 16)
        bb_xn = sem["bb"].n
        for e in range(3):
            fn = [AF.Tanh, AF.Gelu, AF.Silu][e]
            for hc in range(32):
                si = slab_idx
                sb4 = si % 4
                if si >= 4:
                    p.wge(GE, sem["t4"], slab_cons[si - 4])
                p.inc(gss[sb4], GE.dma_start(
                    out=slab[:, sb4].rearrange("p a b -> p (a b)"),
                    in_=w1s[e, hc].rearrange("p a b -> p (a b)")), 16)
                slab_idx += 1
                gidx = qc * 96 + e * 32 + hc
                p.wge(TE, gss[sb4], 16 * (si // 4 + 1))
                if gidx == 0:
                    p.wge(TE, sem["a4"], a4_wT)
                p.wge(TE, sem["bb"], bb_xn)
                if gidx >= 2:
                    p.wge(TE, sem["a4"], a4_act.get(gidx - 2, 0))
                if qc > 0 and gidx < 96 + 2:
                    p.wge(TE, sem["a4"], a4_ev.get((qc - 1, 30 + gidx - 96), 0))
                last = None
                for dc in range(32):
                    last = TE.matmul(mm_ps[gidx % 2][:], slab[:, sb4, dc, :],
                                     bigbuf[:, dc, :], start=(dc == 0), stop=(dc == 31))
                p.inc(sem["t4"], last, 1)
                slab_cons[si] = sem["t4"].n
                t4_w1[gidx] = sem["t4"].n
                p.wge(SE, sem["t4"], t4_w1[gidx])
                ins = SE.activation(acts[:, e, hc, :], mm_ps[gidx % 2][:], fn,
                                    bias=g_b1[:, e * 32 + hc:e * 32 + hc + 1])
                p.inc(sem["a4"], ins, 1)
                a4_act[gidx] = sem["a4"].n
        p.wge(TE, sem["v4"], v4_wTb)
        if qc > 0:
            p.wge(TE, sem["v4"], v4_dcg.get((qc - 1, 7), 0))
        for e in range(3):
            p.wge(TE, sem["a4"], a4_wb)
            p.inc(sem["t4"], TE.matmul(wb_ps[:], g_ones_row[:],
                                       wgtT[e][0:1, qc * 512:(qc + 1) * 512],
                                       start=True, stop=True), 1)
            p.wge(SE, sem["t4"], sem["t4"].n)
            p.inc(sem["a4"], SE.copy(wgtb[e][:], wb_ps[:]), 1)
            a4_wb = sem["a4"].n
        p.wge(VE, sem["a4"], a4_wb)
        p.wge(VE, sem["a4"], a4_act[qc * 96 + 95])
        ins = None
        for e in range(3):
            for hc in range(32):
                ins = VE.tensor_tensor(acts[:, e, hc, :], acts[:, e, hc, :],
                                       wgtb[e][:], ALU.mult)
        p.inc(sem["v4"], ins, 1)
        v4_presc = sem["v4"].n
        p.wge(GE, sem["t4"], t4_w1[qc * 96 + 95])
        p.wge(GE, sem["cc"], 7)
        for rr in range(8):
            static_off = rr * 128 * S + qc * 512
            GE.reg_mov(ago_rtmp, static_off)
            GE.reg_add(ago_rtmp, ago_rtmp, ago_rbase)
            src = bass.AP(ago_out, ago_rtmp,
                          [[S, 128], [NCORES * 128 * S, 4], [1, 512]])
            p.inc(sem["bbg"], GE.dma_start(
                out=bigbuf.rearrange("p (r q) t -> p q r t", q=4)[:, :, rr, :]
                if False else
                bass.AP(bigbuf, rr * 4 * 512,
                        [[32 * 512, 128], [512, 4], [1, 512]]),
                in_=src), 16)
        bb_ago = sem["bbg"].n
        for dc in range(32):
            b2p = dc % 2
            p.wge(GE, sem["t4"], b2_cons.get((qc, dc - 2), 0))
            p.inc(sem["rG"], GE.dma_start(
                out=b2sl[0:1, b2p],
                in_=b2g.rearrange("o (e q) -> o e q", e=3)[
                    0:1, :, dc * 128:(dc + 1) * 128]), 16)
            rg_b2 = sem["rG"].n
            srcs = [wos[dc]] + [w2s[e, dc] for e in range(3)]
            sis = []
            for src_w in srcs:
                si = slab_idx
                sb4 = si % 4
                if si >= 4:
                    p.wge(GE, sem["t4"], slab_cons[si - 4])
                p.inc(gss[sb4], GE.dma_start(
                    out=slab[:, sb4].rearrange("p a b -> p (a b)"),
                    in_=src_w.rearrange("p a b -> p (a b)")), 16)
                sis.append(si)
                slab_idx += 1
            acc = mm_ps[dc % 2]
            if dc >= 2:
                p.wge(TE, sem["a4"], a4_ev.get((qc, dc - 2), 0))
            else:
                p.wge(TE, sem["a4"], a4_act[qc * 96 + 94 + (dc % 2)])
            for k4, si in enumerate(sis):
                sb4 = si % 4
                p.wge(TE, gss[sb4], 16 * (si // 4 + 1))
                if k4 == 0:
                    p.wge(TE, sem["bbg"], bb_ago)
                else:
                    p.wge(TE, sem["v4"], v4_presc)
                last = None
                for c in range(32):
                    rhs = bigbuf[:, c, :] if k4 == 0 else acts[:, k4 - 1, c, :]
                    last = TE.matmul(acc[:], slab[:, sb4, c, :], rhs,
                                     start=(k4 == 0 and c == 0), stop=False)
                p.inc(sem["t4"], last, 1)
                slab_cons[si] = sem["t4"].n
            p.wge(TE, sem["rG"], rg_b2)
            ins = None
            for e in range(3):
                ins = TE.matmul(acc[:], b2sl[0:1, b2p, e, :],
                                wgtTb[e][0:1, qc * 512:(qc + 1) * 512],
                                start=False, stop=(e == 2))
            p.inc(sem["t4"], ins, 1)
            b2_cons[(qc, dc)] = sem["t4"].n
            t4_mm = sem["t4"].n
            p.wge(SE, sem["t4"], t4_mm)
            p.wge(SE, sem["t4"], t4_tp.get((qc, dc - 2), 0))
            p.inc(sem["a4"], SE.copy(stge[:, dc % 2, :], acc[:]), 1)
            a4_ev[(qc, dc)] = sem["a4"].n
            dcg, dslot = dc // 4, dc % 4
            p.wge(TE, sem["a4"], a4_ev[(qc, dc)])
            if dc >= 4:
                p.wge(TE, sem["v4"], v4_dcg.get((qc, dcg - 1), 0))
            elif qc > 0:
                p.wge(TE, sem["v4"], v4_dcg.get((qc - 1, 7), 0))
            lastt = None
            for tt4 in range(4):
                lastt = TE.transpose(pst[tt4][:, dslot * 128:(dslot + 1) * 128],
                                     stge[:, dc % 2, tt4 * 128:(tt4 + 1) * 128],
                                     g_identf[:])
            p.inc(sem["t4"], lastt, 1)
            t4_tp[(qc, dc)] = sem["t4"].n
            if dslot == 0:
                prev = v4_dcg.get((qc, dcg - 1), v4_dcg.get((qc - 1, 7), 0))
                p.wge(SY, sem["v4"], prev)
                for tt4 in range(4):
                    p.inc(sem["sxe"], SY.dma_start(
                        out=xsl[:, tt4, :],
                        in_=x[qc * 512 + tt4 * 128:qc * 512 + (tt4 + 1) * 128,
                              dcg * 512:(dcg + 1) * 512]), 16)
                xsl_load[(qc, dcg)] = sem["sxe"].n
            if dslot == 3:
                p.wge(VE, sem["t4"], t4_tp[(qc, dc)])
                p.wge(VE, sem["sxe"], xsl_load[(qc, dcg)])
                for tt4 in range(4):
                    op = os_seq % 2
                    p.wge(VE, oss[op], 16 * (os_seq // 2))
                    ins = VE.tensor_tensor(ostg[:, op, :], pst[tt4][:],
                                           xsl[:, tt4, :], ALU.add)
                    p.inc(sem["v4"], ins, 1)
                    p.wge(SY, sem["v4"], sem["v4"].n)
                    p.inc(oss[op], SY.dma_start(
                        out=out[qc * 512 + tt4 * 128:qc * 512 + (tt4 + 1) * 128,
                                dcg * 512:(dcg + 1) * 512],
                        in_=ostg[:, op, :]), 16)
                    os_seq += 1
                v4_dcg[(qc, dcg)] = sem["v4"].n
        t4_dout_last = sem["t4"].n
    p.wge(SY, oss[0], oss[0].n)
    p.wge(SY, oss[1], oss[1].n)
    if sem["dbgs"].n:
        p.wge(SY, sem["dbgs"], sem["dbgs"].n)

    nc._es = es
    return nc, dbg_t


_PERM = np.concatenate([np.arange(0, 128, 2), np.arange(1, 128, 2)])


def _prep_inputs(L0_attn_x, L0_attn_wq, L0_attn_wk, L0_attn_wv, L0_attn_wo,
                 L0_ffn_gate_w, L0_ffn_gate_b, L0_ffn_w1, L0_ffn_b1,
                 L0_ffn_w2, L0_ffn_b2):
    f = np.float32
    x2 = np.asarray(L0_attn_x, f).reshape(S, D)
    wq = np.asarray(L0_attn_wq, f)
    wk = np.asarray(L0_attn_wk, f)
    wv = np.asarray(L0_attn_wv, f)
    wo = np.asarray(L0_attn_wo, f)
    gw = np.asarray(L0_ffn_gate_w, f)
    gb = np.asarray(L0_ffn_gate_b, f)
    w1 = np.asarray(L0_ffn_w1, f)
    b1 = np.asarray(L0_ffn_b1, f)
    w2 = np.asarray(L0_ffn_w2, f)
    b2 = np.asarray(L0_ffn_b2, f)

    wqp = wq.reshape(D, 32, 128)[:, :, _PERM].reshape(32, 128, 32, 128)
    wq_h = np.ascontiguousarray(wqp.transpose(1, 0, 2, 3))
    wkp = wk.reshape(D, 8, 128)[:, :, _PERM].reshape(32, 128, 8, 128)
    wk_h = np.ascontiguousarray(wkp.transpose(1, 0, 2, 3))
    wv_h = np.ascontiguousarray(wv.reshape(32, 128, 8, 128).transpose(1, 0, 2, 3))
    w1s = np.ascontiguousarray(w1.reshape(3, 32, 128, 32, 128).transpose(0, 3, 2, 1, 4))
    w2s = np.ascontiguousarray(w2.reshape(3, 32, 128, 32, 128).transpose(0, 3, 2, 1, 4))
    wos = np.ascontiguousarray(wo.reshape(32, 128, 32, 128).transpose(2, 1, 0, 3))
    gwh = np.ascontiguousarray(gw[:D].reshape(32, 128, 3).transpose(1, 0, 2))
    gwt0 = np.ascontiguousarray(gw[:D].T.reshape(1, 3 * D))
    gwlast = np.ascontiguousarray(gw[D:D + 1, :])
    gbin = np.ascontiguousarray(gb.reshape(1, 3))
    b1h = np.ascontiguousarray(b1.reshape(3, 32, 128).transpose(2, 0, 1).reshape(128, 96))
    b2g = np.ascontiguousarray(b2.reshape(1, 3 * D))

    i64 = np.arange(64)
    freqs = (10000.0 ** (-(2.0 * i64) / 128.0)).astype(np.float64)
    ang = np.outer(np.arange(S, dtype=np.float64), freqs)
    c = np.cos(ang).T.astype(f)
    s = np.sin(ang).T.astype(f)
    ctab = np.ascontiguousarray(np.concatenate([c, c], 0))
    stab = np.ascontiguousarray(np.concatenate([-s, s], 0))

    pp = np.arange(128)[:, None]
    jj = np.arange(512)[None, :]
    maskh = np.stack([(kt * 128 + pp <= jj) for kt in range(4)], axis=1)
    maskh = np.ascontiguousarray(maskh.astype(ml_dtypes.bfloat16))
    ident = np.ascontiguousarray(np.eye(128, dtype=ml_dtypes.bfloat16))
    identf = np.ascontiguousarray(np.eye(128, dtype=f))

    in_maps = []
    for r in range(NCORES):
        tok0 = r * TOK
        vm = ((tok0 + np.arange(NT)[None, :] * 128 + np.arange(128)[:, None])
              < VIS).astype(f)
        in_maps.append({
            "x": np.ascontiguousarray(x2[tok0:tok0 + TOK]),
            "wqh": np.ascontiguousarray(wq_h[:, :, 4 * r:4 * r + 4, :]),
            "wkh": np.ascontiguousarray(wk_h[:, :, r, :]),
            "wvh": np.ascontiguousarray(wv_h[:, :, r, :]),
            "w1s": w1s, "w2s": w2s, "wos": wos,
            "gwh": gwh, "gwt0": gwt0, "gwlast": gwlast, "gbin": gbin,
            "b1h": b1h, "b2g": b2g,
            "ctabin": ctab, "stabin": stab, "maskin": maskh,
            "vmaskin": np.ascontiguousarray(vm),
            "identin": ident, "identfin": identf,
            "is0in": np.array([[1.0 if r == 0 else 0.0]], f),
            "agoffin": np.array([[r * TOK]], np.int32),
        })
    return in_maps


def kernel(**inputs):
    in_maps = _prep_inputs(**inputs)
    nc, _ = build()
    res = run_bass_kernel_spmd(nc, in_maps, core_ids=list(range(NCORES)))
    outs = [np.asarray(res.results[r]["out"], np.float32) for r in range(NCORES)]
    return np.concatenate(outs, axis=0).reshape(1, S, D)
